# version 16
# baseline (speedup 1.0000x reference)
import numpy as np
import ml_dtypes
import concourse.bass as bass
import concourse.mybir as mybir
from concourse.bass_utils import run_bass_kernel_spmd

F32 = mybir.dt.float32
BF16 = mybir.dt.bfloat16
I32 = mybir.dt.int32
AF = mybir.ActivationFunctionType
ALU = mybir.AluOpType
AX = mybir.AxisListType

T = 2048
D = 1024
CH = 8
NE = 8
PART = "full"

_XBITS = [3229614080, 3228565504, 3227516928, 3226468352, 3225419776, 3224371200,
          3223322624, 3222274048, 3221225472, 3219128320, 3217031168, 3214934016,
          3212836864, 3208642560, 3204448256, 3196059648, 0, 1048576000, 1056964608,
          1061158912, 1065353216, 1067450368, 1069547520, 1071644672, 1073741824,
          1074790400, 1075838976, 1076887552, 1077936128, 1078984704, 1080033280,
          1081081856, 1082130432]
_YBITS = [3104096256, 3115170560, 3126160128, 3136671360, 3146036032, 3154526592,
          3162396960, 3168883268, 3174719104, 3180302958, 3184343880, 3188144980,
          3189929606, 3190688958, 3189635234, 3184359005, 0, 1041843410, 1051788207,
          1058307280, 1062691422, 1066342549, 1068706892, 1071056601, 1073360140,
          1074675036, 1075773863, 1076853182, 1077919142, 1078976838, 1080029866,
          1081080467, 1082129901]


def _pwl_terms():
    X = np.array(_XBITS, np.uint32).view(np.float32).astype(np.float64)
    Y = np.array(_YBITS, np.uint32).view(np.float32).astype(np.float64)
    s = (Y[1:] - Y[:-1]) / 0.25
    c = np.empty(33)
    c[0] = s[0]
    c[1:32] = s[1:] - s[:-1]
    c[32] = 1.0 - s[31]
    U = np.concatenate([X[:32] * 256.0, [1024.0]])
    terms = [(float(c[i]), float(U[i]), +1.0) for i in range(33)]
    d = 256.0 * Y[0]
    terms += [(-d, 1024.0, +1.0), (d, 1025.0, +1.0),
              (-d, 1024.0, -1.0), (d, 1025.0, -1.0)]
    const = 256.0 * Y[0] + 0.5
    return terms, const

PWL_TERMS, PWL_CONST = _pwl_terms()

def _pwl_terms_ge():
    X = np.array(_XBITS, np.uint32).view(np.float32).astype(np.float64)
    Y = np.array(_YBITS, np.uint32).view(np.float32).astype(np.float64)
    t = X[16:]
    G = Y[16:] - t / 2.0
    sG = (G[1:] - G[:-1]) / 0.25
    cG = np.empty(17)
    cG[0] = sG[0]
    cG[1:16] = sG[1:] - sG[:-1]
    cG[16] = 0.5 - sG[15]
    UG = np.concatenate([t[:16] * 256.0, [1024.0]])
    dstep = 256.0 * (4.0 - Y[32])
    Ev = np.array([Y[j] - (X[j] / 2.0 + G[16 - j]) for j in range(9)])
    d = np.zeros(9)
    for i in range(7, -1, -1):
        s = sum(d[j] * (X[j] - X[i]) for j in range(i + 1, 9))
        d[i + 1] = (Ev[i] - s) / (X[i + 1] - X[i])
    terms = []
    for i in range(17):
        terms.append(("a", float(cG[i]), float(UG[i])))
    terms.append(("a", float(dstep), 1024.0))
    terms.append(("a", float(-dstep), 1025.0))
    for j in range(1, 9):
        terms.append(("u", float(256.0 * d[j]), float(-256.0 * X[j])))
    cjs = [(c, U) for (s, c, U) in terms if s == "u"]
    def Elr(u):
        return sum(c * max(-u - U, 0.0) for c, U in cjs)
    v1025 = Elr(-1025.0)
    v1026 = Elr(-1026.0)
    m = v1026 - v1025
    p = -v1025
    q = -v1026 - 2.0 * p
    terms.append(("u", float(p), 1024.0))
    terms.append(("u", float(q), 1025.0))
    def ge_y(u):
        acc = 0.5 + 0.5 * u
        for (s, c, U) in terms:
            acc += c * max((abs(u) if s == "a" else -u) - U, 0.0)
        return acc
    tref, cref = _pwl_terms()
    yref = cref + sum(c * max(sd * -740.0 - U, 0.0) for (c, U, sd) in tref)
    dfix = yref - ge_y(-740.0)
    return terms, float(dfix)

PWL_GE, PWL_DFIX = _pwl_terms_ge()
PWL_POOL = frozenset((1, 3, 6, 8, 11, 13, 16, 18, 21, 23, 26, 28))



class Chan:
    def __init__(self, sem, inc):
        self.sem = sem
        self.inc = inc
        self.count = 0


class Tile:
    __slots__ = ("h", "name", "wr", "rd")
    def __init__(self, h, name):
        self.h = h
        self.name = name
        self.wr = None
        self.rd = {}
    def __getitem__(self, idx):
        return self.h[idx]


class Bld:
    ENGS = ("tensor", "vector", "scalar", "gpsimd", "sync")

    def __init__(self, nc):
        self.nc = nc
        self._cm = []
        self.q = {e: [] for e in self.ENGS}
        self.ch = {}
        for name, inc in [("t", 1), ("v", 1), ("a", 1), ("p", 1),
                          ("qs0", 16), ("qs1", 16), ("qs2", 16), ("qs3", 16),
                          ("w0", 16), ("w1", 16), ("w2", 16), ("w3", 16), ("cc", 1)]:
            cm = nc.semaphore("ch_" + name)
            sem = cm.__enter__()
            self._cm.append(cm)
            self.ch[name] = Chan(sem, inc)
        self.clock = {}
        self._eng2ch = {"tensor": "t", "vector": "v", "scalar": "a", "gpsimd": "p"}
        self._wrot = 0
        self._qrot = 0
        self.nins = 0

    def sb(self, name, p, f, dt):
        return Tile(self.nc.alloc_sbuf_tensor(name, [p, f], dt), name)

    def ps(self, name, p, f, dt=F32):
        return Tile(self.nc.alloc_psum_tensor(name, [p, f], dt), name)

    def _wait(self, eng, chkey, count):
        if count <= 0:
            return
        if eng == "tensor" and chkey == "t":
            return
        key = (eng, chkey)
        if self.clock.get(key, 0) >= count:
            return
        self.clock[key] = count
        sem = self.ch[chkey].sem
        self.q[eng].append(lambda h, s=sem, c=count: h.wait_ge(s, c))

    def _pre(self, eng, reads, writes):
        for t in reads:
            if t.wr is not None:
                self._wait(eng, t.wr[0], t.wr[1])
        for t in writes:
            if t.wr is not None:
                self._wait(eng, t.wr[0], t.wr[1])
            for ck, cnt in t.rd.items():
                self._wait(eng, ck, cnt)

    def _post(self, chkey, reads, writes):
        ch = self.ch[chkey]
        ch.count += ch.inc
        for t in reads:
            t.rd[chkey] = ch.count
        for t in writes:
            t.wr = (chkey, ch.count)
            t.rd = {}

    def op(self, eng, fn, reads=(), writes=(), chkey=None):
        ck = chkey or self._eng2ch[eng]
        self._pre(eng, list(reads), list(writes))
        self._post(ck, list(reads), list(writes))
        sem, icr = self.ch[ck].sem, self.ch[ck].inc
        self.q[eng].append(lambda h, f=fn, s=sem, c=icr: f(h).then_inc(s, c))
        self.nins += 1

    def grp(self, eng, fns, reads=(), writes=(), chkey=None):
        ck = chkey or self._eng2ch[eng]
        self._pre(eng, list(reads), list(writes))
        self._post(ck, list(reads), list(writes))
        sem, icr = self.ch[ck].sem, self.ch[ck].inc
        for f in fns[:-1]:
            self.q[eng].append(f)
        self.q[eng].append(lambda h, f=fns[-1], s=sem, c=icr: f(h).then_inc(s, c))
        self.nins += len(fns)

    def dma(self, qeng, chkey, dst_ap, src_ap, reads=(), writes=()):
        if chkey == "qs":
            chkey = "qs" + str(self._qrot % 4)
            self._qrot += 1
        self._pre(qeng, list(reads), list(writes))
        self._wait(qeng, chkey, self.ch[chkey].count)
        self._post(chkey, list(reads), list(writes))
        sem = self.ch[chkey].sem
        self.q[qeng].append(
            lambda h, o=dst_ap, i=src_ap, s=sem: h.dma_start(o, i).then_inc(s, 16))
        self.nins += 1

    def wload(self, dst_tile, src_ap, dst_ap=None):
        ck = "w" + str(self._wrot % 4)
        self._wrot += 1
        self.dma("gpsimd", ck,
                 dst_ap if dst_ap is not None else dst_tile[:, :], src_ap,
                 writes=[dst_tile])

    def barrier(self, eng, chkey):
        self._wait(eng, chkey, self.ch[chkey].count)

    def emit(self):
        with self.nc.Block() as block:
            for e in self.ENGS:
                if not self.q[e]:
                    continue
                dec = getattr(block, e)
                def section(h, q=tuple(self.q[e])):
                    for f in q:
                        f(h)
                dec(section)


def _const_layout():
    cvals = []
    def C(v):
        cvals.append(float(v))
        return len(cvals) - 1
    C_EXPS = C(-20.0)
    C_G8 = C(-8.0)
    C_INV256 = C(2.0 ** -8)
    C_EPS = C(1e-5)
    pwl_cols = []
    for (srt, c, U) in PWL_GE:
        pwl_cols.append((C(abs(c)), C(-abs(c) * U), 1.0 if c > 0 else -1.0))
    return cvals, C_EXPS, C_G8, C_INV256, C_EPS, pwl_cols


VB_OUT = 0
VB2L0 = 8
VGAM1 = 16
VSHB = 24
VGAMF = 32
VBETF = 40


def build_program(part, stop=None):
    nc = bass.Bass("TRN2", target_bir_lowering=False)
    B = Bld(nc)
    do_l0 = part in ("l0", "full")
    do_l1 = part in ("l1", "full")

    cvals, C_EXPS, C_G8, C_INV256, C_EPS, pwl_cols = _const_layout()
    NCON = max(64, len(cvals))

    din = lambda n, s, dt=F32: nc.dram_tensor(n, list(s), dt, kind="ExternalInput")
    dout = lambda n, s, dt=F32: nc.dram_tensor(n, list(s), dt, kind="ExternalOutput")
    dint = lambda n, s, dt=F32: nc.dram_tensor(n, list(s), dt, kind="Internal")

    if do_l0:
        x_in = din("xown", (CH, 128, T))
        xoth = din("xoth", (CH, 128, T))
        wq_g = din("wq_g", (4, 8, 128, 128))
        wkvT = din("wkvT", (8, 128, 1024))
        woutT_g = din("woutT_g", (4, 128, 1024))
        w1_g = din("w1_g", (32, 8, 128, 128))
        w2_g = din("w2_g", (32, 128, 1024))
        b1l0_in = din("b1l0", (128, 32))
    if do_l1:
        qkva_g = din("qkva_g", (24, 8, 128, 128))
        gate_g = din("gate_g", (8, 128, 8))
        shared_g = din("shared_g", (8, 128, 1024))
        w1e_g = din("w1e_g", (NE, 32, 8, 128, 128))
        w2e_g = din("w2e_g", (NE, 32, 128, 1024))
        b1e_in = din("b1e", (128, 256))
        b2c_in = din("b2c", (128, 64))
        role_in = din("role", (128, 1))
        carry_in = dint("cc_in", (1, 1024))
        carry_out = dint("cc_out", (2, 1024))
        cb_d = dint("cb_d", (8, 1024))
        ssp = dint("sspill", (CH, 128, T))
        out_d = dout("out", (CH, 128, T))
    vecs_in = din("vecs", (128, 64))
    consts_in = din("consts", (128, NCON))
    ident_in = din("ident", (128, 128))
    if part == "l0":
        x1_d = dout("x1", (CH, 128, T))
    elif part == "l1":
        x1_d = din("x1", (CH, 128, T))
    else:
        x1_d = dint("x1", (CH, 128, T))

    sb, ps = B.sb, B.ps
    A = [sb(f"A{i}", 128, 1024, F32) for i in range(8)]
    Cp = [sb(f"C{i}", 128, 1024, F32) for i in range(8)]
    Dp = [sb(f"D{i}", 128, 1024, BF16) for i in range(16)]
    xnbf = [sb(f"xn{i}", 128, T, BF16) for i in range(CH)]
    wg = [sb(f"wg{i}", 128, 1024, BF16) for i in range(4)]
    ws = [sb(f"ws{i}", 128, 1024, BF16) for i in range(8)]
    r_m = sb("r_m", 1, 512, F32)
    r_v = sb("r_v", 1, 512, F32)
    r_rstd = sb("r_rstd", 1, 512, F32)
    r_mrstd = sb("r_mrstd", 1, 512, F32)
    r_rsqv = sb("r_rsqv", 1, 512, F32)
    r_mrsqv = sb("r_mrsqv", 1, 512, F32)
    r_t0 = sb("r_t0", 1, 512, F32)
    onesrow_f = sb("onesrow_f", 1, 128, F32)
    onesrow_bf = sb("onesrow_bf", 1, 512, BF16)
    onescol_f = sb("onescol_f", 128, 1, F32)
    svecs = sb("svecs", 128, 64, F32)
    sconst = sb("sconst", 128, NCON, F32)
    sident_f = sb("sident_f", 128, 128, F32)
    zti = sb("zti", 128, 1024, I32)
    pacc = Cp[7]
    ozi = sb("ozi", 128, 512, I32)
    oz = sb("oz", 128, 512, F32)
    ou0 = sb("ou0", 128, 512, F32)
    od = sb("od", 128, 512, F32)
    ot1 = sb("ot1", 128, 512, F32)
    oq = sb("oq", 128, 512, F32)
    og1 = sb("og1", 128, 512, F32)
    ek = sb("ek", 128, 512, F32)
    krec = sb("krec", 128, 8, F32)
    kbf = sb("kbf", 128, 512, BF16)
    vbf = sb("vbf", 128, 512, BF16)

    ctx_bd = sb("ctx_bd", 128, 512, BF16)
    qbf = [sb(f"qbf{m}", 128, 512, BF16) for m in range(4)]
    aobf = [sb(f"aobf{p}", 128, 512, BF16) for p in range(4)]
    if do_l0:
        b1cc = sb("b1cc", 128, 32, F32)
        b2cc = sb("b2cc", 128, 8, F32)
    if do_l1:
        b1e_s = sb("b1e_s", 128, 256, F32)
        b2c_s = sb("b2c_s", 128, 64, F32)
        role_sb = sb("role_sb", 128, 1, F32)
        carry_sb = sb("carry_sb", 128, 8, F32)
        carryin_sb = sb("carryin_sb", 128, 8, F32)
        comb = [sb(f"comb{tc}", 128, 8, F32) for tc in range(8)]
        cbstage = sb("cbstage", 8, 1024, F32)
        cbrow_bf = sb("cbrow_bf", 1, 1024, BF16)
        cbb = A[3]
        gx = sb("gx", 128, 64, BF16)
        p8 = sb("p8", 128, 8, F32)
        p8b = sb("p8b", 128, 8, F32)
        m1c = sb("m1c", 128, 1, F32)
        m2c = sb("m2c", 128, 1, F32)
        gf = A[4]
        gf2 = A[5]
    PS = [ps(f"ps{i}", 128, 1024) for i in range(4)]

    def cc(i):
        return sconst[:, i:i + 1]

    class DTile:
        __slots__ = ("name", "wr", "rd")
        def __init__(self, name):
            self.name = name
            self.wr = None
            self.rd = {}
    dT = {}
    def dt(name):
        if name not in dT:
            dT[name] = DTile(name)
        return dT[name]

    B.dma("sync", "qs", svecs[:, :], vecs_in[:, :], writes=[svecs])
    B.dma("sync", "qs", sconst[:, :], consts_in[:, :], writes=[sconst])
    B.dma("sync", "qs", sident_f[:, :], ident_in[:, :], writes=[sident_f])
    B.op("gpsimd", lambda h: h.memset(onesrow_f[:, :], 1.0), writes=[onesrow_f])
    B.op("gpsimd", lambda h: h.memset(onesrow_bf[:, :], 1.0), writes=[onesrow_bf])
    B.op("gpsimd", lambda h: h.memset(onescol_f[:, :], 1.0), writes=[onescol_f])
    if do_l0:
        B.dma("sync", "qs", b1cc[:, :], b1l0_in[:, :], writes=[b1cc])
        B.op("vector", lambda h: h.tensor_scalar(
            b1cc[:, :], b1cc[:, :], 256.0, 0.5, ALU.mult, ALU.add),
            reads=[b1cc], writes=[b1cc])
        B.op("vector", lambda h: h.tensor_scalar(
            b2cc[:, :], svecs[:, VB2L0:VB2L0 + 8], 256.0, 0.5, ALU.mult, ALU.add),
            reads=[svecs], writes=[b2cc])
    if do_l1:
        B.dma("sync", "qs", b1e_s[:, :], b1e_in[:, :], writes=[b1e_s])
        B.dma("sync", "qs", b2c_s[:, :], b2c_in[:, :], writes=[b2c_s])
        B.dma("sync", "qs", role_sb[:, :], role_in[:, :], writes=[role_sb])
        for k in range(8):
            B.dma("gpsimd", "w3", gx[:, k * 8:(k + 1) * 8], gate_g[k], writes=[gx])

    def stats_tile(xts, w, sq_slots, with_rsqv=False):
        tiles = [t for t, _ in xts]
        fns = [lambda h, t=t, sl=sl, i=i: h.matmul(
            PS[3].h[0:1, 0:w], onescol_f[:, :], t.h[sl],
            start=(i == 0), stop=(i == 7)) for i, (t, sl) in enumerate(xts)]
        B.grp("tensor", fns, reads=[onescol_f] + tiles, writes=[PS[3]])
        B.op("vector", lambda h: h.tensor_scalar(
            r_m[0:1, 0:w], PS[3][0:1, 0:w], 1.0 / D, None, ALU.mult),
            reads=[PS[3]], writes=[r_m])
        for i in range(8):
            t, sl = xts[i]
            st, ssl = sq_slots[i]
            B.op("scalar", lambda h, t=t, sl=sl, st=st, ssl=ssl: h.activation(
                st.h[ssl], t.h[sl], AF.Square),
                reads=[t], writes=[st])
        fns = [lambda h, i=i: h.matmul(
            PS[3].h[0:1, 512:512 + w], onescol_f[:, :], sq_slots[i][0].h[sq_slots[i][1]],
            start=(i == 0), stop=(i == 7)) for i in range(8)]
        B.grp("tensor", fns, reads=[onescol_f] + [s for s, _ in sq_slots], writes=[PS[3]])
        B.op("vector", lambda h: h.tensor_tensor(
            r_t0[0:1, 0:w], r_m[0:1, 0:w], r_m[0:1, 0:w], ALU.mult),
            reads=[r_m], writes=[r_t0])
        B.op("vector", lambda h: h.scalar_tensor_tensor(
            r_v[0:1, 0:w], PS[3][0:1, 512:512 + w], 1.0 / D, r_t0[0:1, 0:w],
            ALU.mult, ALU.subtract),
            reads=[PS[3], r_t0], writes=[r_v])
        B.op("scalar", lambda h: h.activation(
            r_rstd[0:1, 0:w], r_v[0:1, 0:w], AF.Sqrt,
            bias=sconst[0:1, C_EPS:C_EPS + 1]),
            reads=[r_v, sconst], writes=[r_rstd])
        B.op("vector", lambda h: h.reciprocal(r_rstd[0:1, 0:w], r_rstd[0:1, 0:w]),
             reads=[r_rstd], writes=[r_rstd])
        B.op("vector", lambda h: h.tensor_tensor(
            r_mrstd[0:1, 0:w], r_m[0:1, 0:w], r_rstd[0:1, 0:w], ALU.mult),
            reads=[r_m, r_rstd], writes=[r_mrstd])
        if with_rsqv:
            B.op("scalar", lambda h: h.activation(
                r_rsqv[0:1, 0:w], r_v[0:1, 0:w], AF.Sqrt),
                reads=[r_v], writes=[r_rsqv])
            B.op("vector", lambda h: h.reciprocal(r_rsqv[0:1, 0:w], r_rsqv[0:1, 0:w]),
                 reads=[r_rsqv], writes=[r_rsqv])
            B.op("vector", lambda h: h.tensor_tensor(
                r_mrsqv[0:1, 0:w], r_m[0:1, 0:w], r_rsqv[0:1, 0:w], ALU.mult),
                reads=[r_m, r_rsqv], writes=[r_mrsqv])

    def bcast2(rowA, rowB, w, bank):
        B.grp("tensor", [lambda h: h.matmul(
            bank.h[:, 0:w], onesrow_f[0:1, 0:128], rowA.h[0:1, 0:w],
            start=True, stop=True)],
            reads=[onesrow_f, rowA], writes=[bank])
        B.grp("tensor", [lambda h: h.matmul(
            bank.h[:, 512:512 + w], onesrow_f[0:1, 0:128], rowB.h[0:1, 0:w],
            start=True, stop=True)],
            reads=[onesrow_f, rowB], writes=[bank])

    def ln_apply(xts, w, outs, bank, gamma_base=None, tmpA=None, tmpB=None):
        for i in range(8):
            xt, xsl = xts[i]
            ot, osl = outs[i]
            B.op("vector", lambda h, xt=xt, xsl=xsl: h.tensor_tensor(
                tmpA.h[:, 0:w], xt.h[xsl], bank.h[:, 0:w], ALU.mult),
                reads=[xt, bank], writes=[tmpA])
            if gamma_base is None:
                B.op("vector", lambda h, ot=ot, osl=osl: h.tensor_tensor(
                    ot.h[osl], tmpA.h[:, 0:w], bank.h[:, 512:512 + w], ALU.subtract),
                    reads=[tmpA, bank], writes=[ot])
            else:
                B.op("vector", lambda h: h.tensor_tensor(
                    tmpB.h[:, 0:w], tmpA.h[:, 0:w], bank.h[:, 512:512 + w], ALU.subtract),
                    reads=[tmpA, bank], writes=[tmpB])
                B.op("vector", lambda h, ot=ot, osl=osl, i=i: h.tensor_scalar(
                    ot.h[osl], tmpB.h[:, 0:w],
                    svecs[:, gamma_base + i:gamma_base + i + 1], None, ALU.mult),
                    reads=[tmpB, svecs], writes=[ot])

    def intfloor(zt, zsl, it, u0, d, t1, outt, osl, w):
        B.op("vector", lambda h: h.tensor_copy(it[:, 0:w], zt.h[zsl]),
             reads=[zt], writes=[it])
        B.op("vector", lambda h: h.tensor_copy(u0[:, 0:w], it[:, 0:w]),
             reads=[it], writes=[u0])
        B.op("vector", lambda h: h.tensor_tensor(d[:, 0:w], zt.h[zsl], u0[:, 0:w],
                                                 ALU.subtract),
             reads=[zt, u0], writes=[d])
        B.op("vector", lambda h: h.scalar_tensor_tensor(
            t1[:, 0:w], d[:, 0:w], 1.0, u0[:, 0:w], ALU.is_gt, ALU.add),
            reads=[d, u0], writes=[t1])
        B.op("vector", lambda h: h.tensor_scalar(
            d[:, 0:w], d[:, 0:w], 0.0, -1.0, ALU.is_lt, ALU.mult),
            reads=[d], writes=[d])
        B.op("vector", lambda h: h.tensor_tensor(outt.h[osl], t1[:, 0:w], d[:, 0:w],
                                                 ALU.add),
             reads=[t1, d], writes=[outt])

    SL512 = (slice(0, 128), slice(0, 512))

    if do_l0:
        for tt in range(4):
            s0 = tt * 512
            for i in range(8):
                B.dma("sync", "qs", A[i][:, 0:512], x_in.ap()[i, :, s0:s0 + 512],
                      writes=[A[i]])
            xts = [(A[i], SL512) for i in range(8)]
            sqs = [(Cp[i], SL512) for i in range(8)]
            stats_tile(xts, 512, sqs)
            bcast2(r_rstd, r_mrstd, 512, PS[2])
            outs = [(xnbf[i], (slice(0, 128), slice(s0, s0 + 512))) for i in range(8)]
            ln_apply(xts, 512, outs, PS[2], tmpA=oz)

        if stop == "ph1":
            for i in range(8):
                B.dma("gpsimd", "w3", x1_d.ap()[i], xnbf[i][:, :], reads=[xnbf[i]])
            B.barrier("gpsimd", "w3")
            B.emit()
            return nc
        for k in range(8):
            B.wload(ws[k], wkvT[k])

        def kv_chunk(src_tiles, sls, first, last, kb):
            fns = [lambda h, kcc=kcc, kb=kb: h.matmul(
                kb.h[:, 0:512], src_tiles[kcc].h[sls[kcc]], ws[kcc][:, 0:512],
                start=(kcc == 0), stop=(kcc == 7)) for kcc in range(8)]
            B.grp("tensor", fns, reads=list(src_tiles) + ws[:8], writes=[kb])
            fns = [lambda h, kcc=kcc, kb=kb: h.matmul(
                kb.h[:, 512:1024], src_tiles[kcc].h[sls[kcc]], ws[kcc][:, 512:1024],
                start=(kcc == 0), stop=(kcc == 7)) for kcc in range(8)]
            B.grp("tensor", fns, reads=list(src_tiles) + ws[:8], writes=[kb])
            B.op("scalar", lambda h, kb=kb: h.activation(ek[:, :], kb.h[:, 0:512], AF.Exp,
                                                  bias=cc(C_EXPS)),
                 reads=[kb, sconst], writes=[ek])
            B.op("vector", lambda h: h.tensor_reduce(
                krec[:, :], ek.h.ap().rearrange("p (hh ww) -> p hh ww", hh=8),
                axis=AX.X, op=ALU.add),
                reads=[ek], writes=[krec])
            B.op("vector", lambda h: h.reciprocal(krec[:, :], krec[:, :]),
                 reads=[krec], writes=[krec])
            for hh in range(8):
                B.op("vector", lambda h, hh=hh: h.tensor_scalar(
                    kbf[:, hh * 64:(hh + 1) * 64], ek[:, hh * 64:(hh + 1) * 64],
                    krec[:, hh:hh + 1], None, ALU.mult),
                    reads=[ek, krec], writes=[kbf])
            B.op("scalar", lambda h, kb=kb: h.activation(vbf[:, :], kb.h[:, 512:1024], AF.Copy),
                 reads=[kb], writes=[vbf])
            fns = [lambda h, hh=hh: h.matmul(
                PS[1].h[0:64, hh * 64:(hh + 1) * 64], kbf[:, hh * 64:(hh + 1) * 64],
                vbf[:, hh * 64:(hh + 1) * 64],
                start=(first and hh == 0), stop=(last and hh == 7)) for hh in range(8)]
            B.grp("tensor", fns, reads=[kbf, vbf], writes=[PS[1]])

        for tc in range(16):
            sl = (slice(0, 128), slice(tc * 128, (tc + 1) * 128))
            kv_chunk(xnbf, [sl] * 8, tc == 0, False, PS[0] if tc % 2 == 0 else PS[2])
        for og in range(4):
            for i in range(8):
                B.dma("sync", "qs", A[i][:, 0:512], xoth.ap()[i, :, og * 512:(og + 1) * 512],
                      writes=[A[i]])
            xts = [(A[i], SL512) for i in range(8)]
            sqs = [(Cp[i], SL512) for i in range(8)]
            stats_tile(xts, 512, sqs)
            bcast2(r_rstd, r_mrstd, 512, PS[2])
            outs = [(Dp[i], SL512) for i in range(8)]
            ln_apply(xts, 512, outs, PS[2], tmpA=oz)
            for tc in range(4):
                sl = (slice(0, 128), slice(tc * 128, (tc + 1) * 128))
                kv_chunk(Dp[:8], [sl] * 8, False, og == 3 and tc == 3,
                         PS[0] if tc % 2 == 0 else PS[3])
        B.op("vector", lambda h: h.tensor_copy(oq.h[0:64, 0:512], PS[1][0:64, 0:512]),
             reads=[PS[1]], writes=[oq])
        B.op("vector", lambda h: h.memset(ctx_bd[:, :], 0.0), writes=[ctx_bd])
        for pr in range(4):
            B.op("vector", lambda h, pr=pr: h.tensor_copy(
                ctx_bd.h[0:64, pr * 128:pr * 128 + 64],
                oq.h[0:64, 2 * pr * 64:2 * pr * 64 + 64]),
                reads=[oq], writes=[ctx_bd])
            B.op("vector", lambda h, pr=pr: h.tensor_copy(
                ctx_bd.h[64:128, pr * 128 + 64:pr * 128 + 128],
                oq.h[0:64, (2 * pr + 1) * 64:(2 * pr + 1) * 64 + 64]),
                reads=[oq], writes=[ctx_bd])

        if stop == "ph2":
            B.op("vector", lambda h: h.tensor_copy(A[0][:, 0:512], ctx_bd[:, :]),
                 reads=[ctx_bd], writes=[A[0]])
            B.dma("sync", "qs", x1_d.ap()[0], A[0][:, 0:1024], reads=[A[0]])
            for qq in range(4):
                B.barrier("sync", "qs" + str(qq))
            B.emit()
            return nc
        for m in range(4):
            B.wload(ws[m], wq_g.ap()[m].rearrange("k p j -> p k j"), dst_ap=ws[m].h[:, :].rearrange("p (k j) -> p k j", k=8))
        for k in range(4):
            B.wload(ws[4 + k], woutT_g[k])

        NTG = 1 if stop == "tg1" else 4
        for tg in range(NTG):
            s0 = tg * 512
            def emit_outfloor(pair, zt):
                for hf in range(2):
                    hsl = slice(hf * 512, hf * 512 + 512)
                    B.op("vector", lambda h, zt=zt, hsl=hsl: h.tensor_copy(
                        ozi[:, 0:512], zt[:, hsl]), reads=[zt], writes=[ozi])
                    B.op("vector", lambda h, zt=zt, hsl=hsl: h.tensor_tensor(
                        oz[:, 0:512], zt[:, hsl], ozi[:, 0:512], ALU.subtract),
                        reads=[zt, ozi], writes=[oz])
                    B.op("vector", lambda h: h.scalar_tensor_tensor(
                        od[:, 0:512], oz[:, 0:512], 1.0, ozi[:, 0:512],
                        ALU.is_gt, ALU.add),
                        reads=[oz, ozi], writes=[od])
                    B.op("vector", lambda h, pair=pair, hsl=hsl: h.scalar_tensor_tensor(
                        Dp[pair][:, hsl], oz[:, 0:512], 0.0, od[:, 0:512],
                        ALU.is_lt, ALU.subtract),
                        reads=[oz, od], writes=[Dp[pair]])
            for pair in range(16):
                zt = A[0] if pair % 2 == 0 else Cp[6]
                hbank = PS[0] if pair % 2 == 0 else PS[3]
                for half in range(2):
                    hc = 2 * pair + half
                    wt = wg[hc % 4]
                    B.wload(wt, w1_g.ap()[hc].rearrange("k p j -> p k j"), dst_ap=wt.h[:, :].rearrange("p (k j) -> p k j", k=8))
                    bsl = slice(half * 512, half * 512 + 512)
                    fns = [lambda h, kcc=kcc, wt=wt, bsl=bsl, s0=s0, hbank=hbank: h.matmul(
                        hbank.h[:, bsl], wt[:, kcc * 128:(kcc + 1) * 128],
                        xnbf[kcc][:, s0:s0 + 512], start=(kcc == 0), stop=(kcc == 7))
                        for kcc in range(8)]
                    B.grp("tensor", fns, reads=[wt] + xnbf, writes=[hbank])
                    B.op("vector", lambda h, bsl=bsl, hc=hc, hbank=hbank, zt=zt: h.tensor_scalar(
                        zt[:, bsl], hbank.h[:, bsl], 256.0,
                        b1cc[:, hc:hc + 1], ALU.mult, ALU.add),
                        reads=[hbank, b1cc], writes=[zt])
                B.op("scalar", lambda h, zt=zt: h.activation(
                    zti[:, :], zt[:, :], AF.Copy),
                    reads=[zt], writes=[zti])
                B.op("vector", lambda h, zt=zt: h.tensor_tensor(
                    A[7][:, :], zt[:, :], zti[:, :], ALU.subtract),
                    reads=[zt, zti], writes=[A[7]])
                for hf, mt in ((0, od), (1, ot1)):
                    hsl = slice(hf * 512, hf * 512 + 512)
                    B.op("vector", lambda h, mt=mt, hsl=hsl: h.scalar_tensor_tensor(
                        mt[:, 0:512], A[7][:, hsl], 1.0, zti[:, hsl],
                        ALU.is_gt, ALU.add),
                        reads=[A[7], zti], writes=[mt])
                for hf, mt in ((0, od), (1, ot1)):
                    hsl = slice(hf * 512, hf * 512 + 512)
                    B.op("vector", lambda h, mt=mt, hsl=hsl: h.scalar_tensor_tensor(
                        A[4][:, hsl], A[7][:, hsl], 0.0, mt[:, 0:512],
                        ALU.is_lt, ALU.subtract),
                        reads=[A[7], mt], writes=[A[4]])
                if pair >= 1:
                    emit_outfloor(pair - 1, Cp[6] if pair % 2 == 0 else A[0])
                B.op("scalar", lambda h: h.activation(
                    A[7][:, :], A[4][:, :], AF.Abs),
                    reads=[A[4]], writes=[A[7]])
                B.op("vector", lambda h: h.tensor_scalar(
                    A[6][:, :], A[4][:, :], 740.0, -PWL_DFIX, ALU.is_equal,
                    ALU.mult),
                    reads=[A[4]], writes=[A[6]])
                rrot = (A[1], A[2], A[3])
                dve_pos = False
                pool_sign = 0.0
                for ti, ((srt, _c, _U), (cci, cbi, sgn)) in enumerate(
                        zip(PWL_GE, pwl_cols)):
                    r = rrot[ti % 3]
                    src = A[7] if srt == "a" else A[4]
                    B.op("scalar", lambda h, cci=cci, cbi=cbi, r=r, src=src:
                        h.activation(r[:, :], src[:, :], AF.Relu,
                                     bias=cc(cbi), scale=cc(cci)),
                        reads=[src, sconst], writes=[r])
                    if ti in PWL_POOL:
                        if pool_sign == 0.0:
                            pool_sign = sgn
                            B.op("gpsimd", lambda h, r=r: h.tensor_copy(
                                pacc[:, :], r[:, :]),
                                reads=[r], writes=[pacc])
                        else:
                            op = ALU.add if sgn == pool_sign else ALU.subtract
                            B.op("gpsimd", lambda h, r=r, op=op: h.tensor_tensor(
                                pacc[:, :], pacc[:, :], r[:, :], op),
                                reads=[r], writes=[pacc])
                    elif sgn > 0:
                        if not dve_pos:
                            dve_pos = True
                            B.op("vector", lambda h, r=r: h.tensor_scalar(
                                A[5][:, :], r[:, :], 0.5, None, ALU.add),
                                reads=[r], writes=[A[5]])
                        else:
                            B.op("vector", lambda h, r=r: h.tensor_tensor(
                                A[5][:, :], A[5][:, :], r[:, :], ALU.add),
                                reads=[r], writes=[A[5]])
                    else:
                        B.op("vector", lambda h, r=r: h.tensor_tensor(
                            A[6][:, :], A[6][:, :], r[:, :], ALU.add),
                            reads=[r], writes=[A[6]])
                B.op("vector", lambda h: h.tensor_tensor(
                    A[3][:, :], A[5][:, :], A[6][:, :], ALU.subtract),
                    reads=[A[5], A[6]], writes=[A[3]])
                pop = ALU.add if pool_sign > 0 else ALU.subtract
                B.op("vector", lambda h, pop=pop: h.tensor_tensor(
                    A[3][:, :], A[3][:, :], pacc[:, :], pop),
                    reads=[pacc], writes=[A[3]])
                B.op("vector", lambda h, zt=zt: h.scalar_tensor_tensor(
                    zt[:, :], A[4][:, :], -0.5, A[3][:, :], ALU.mult, ALU.add),
                    reads=[A[4], A[3]], writes=[zt])
            emit_outfloor(15, Cp[6])
            for g in range(2):
                for hc in range(32):
                    wt = wg[hc % 4]
                    B.wload(wt, w2_g[hc])
                    pairt = Dp[hc // 2]
                    hsl = slice((hc % 2) * 512, (hc % 2) * 512 + 512)
                    fns = []
                    for dq in range(4):
                        dmc = g * 4 + dq
                        bank = PS[1 + dq // 2]
                        bsl = slice((dq % 2) * 512, (dq % 2) * 512 + 512)
                        fns.append((bank, lambda h, wt=wt, dmc=dmc, bank=bank, bsl=bsl,
                                    pairt=pairt, hsl=hsl, hc=hc: h.matmul(
                            bank.h[:, bsl], wt[:, dmc * 128:(dmc + 1) * 128],
                            pairt.h[:, hsl], start=(hc == 0), stop=(hc == 31))))
                    B.grp("tensor", [f for _, f in fns],
                          reads=[wt, pairt], writes=[PS[1], PS[2]])
                for dq in range(4):
                    dmc = g * 4 + dq
                    bank = PS[1 + dq // 2]
                    bsl = slice((dq % 2) * 512, (dq % 2) * 512 + 512)
                    B.op("vector", lambda h, bank=bank, bsl=bsl, dmc=dmc: h.tensor_scalar(
                        oz[:, :], bank.h[:, bsl], -1.0, b2cc[:, dmc:dmc + 1],
                        ALU.mult, ALU.add),
                        reads=[bank, b2cc], writes=[oz])
                    intfloor(oz, SL512, ozi, ou0, od, ot1, oq, SL512, 512)
                    B.op("scalar", lambda h, dmc=dmc: h.activation(
                        Cp[dmc][:, 0:512], oq[:, 0:512], AF.Silu, scale=cc(C_INV256)),
                        reads=[oq, sconst], writes=[Cp[dmc]])
            for mc in range(4):
                bank = PS[0 if mc < 2 else 3]
                bsl = slice((mc % 2) * 512, (mc % 2) * 512 + 512)
                fns = [lambda h, kcc=kcc, mc=mc, bank=bank, bsl=bsl, s0=s0: h.matmul(
                    bank.h[:, bsl], ws[mc][:, kcc * 128:(kcc + 1) * 128],
                    xnbf[kcc][:, s0:s0 + 512], start=(kcc == 0), stop=(kcc == 7))
                    for kcc in range(8)]
                B.grp("tensor", fns, reads=[ws[mc]] + xnbf, writes=[bank])
                B.op("scalar", lambda h, mc=mc, bank=bank, bsl=bsl: h.activation(
                    qbf[mc][:, :], bank.h[:, bsl], AF.Copy),
                    reads=[bank], writes=[qbf[mc]])
            for pr in range(4):
                bank = PS[0 if pr < 2 else 3]
                bsl = slice((pr % 2) * 512, (pr % 2) * 512 + 512)
                B.grp("tensor", [lambda h, pr=pr, bank=bank, bsl=bsl: h.matmul(
                    bank.h[:, bsl], ctx_bd[:, pr * 128:(pr + 1) * 128], qbf[pr][:, :],
                    start=True, stop=True)],
                    reads=[ctx_bd, qbf[pr]], writes=[bank])
                B.op("scalar", lambda h, pr=pr, bank=bank, bsl=bsl: h.activation(
                    aobf[pr][:, :], bank.h[:, bsl], AF.Copy),
                    reads=[bank], writes=[aobf[pr]])
            for dmc in range(8):
                bank = PS[1 + dmc % 2]
                bsl = slice(0, 512) if dmc < 4 else slice(512, 1024)
                fns = [lambda h, kcc=kcc, dmc=dmc, bank=bank, bsl=bsl: h.matmul(
                    bank.h[:, bsl], ws[4 + kcc][:, dmc * 128:(dmc + 1) * 128],
                    aobf[kcc][:, :], start=(kcc == 0), stop=(kcc == 3))
                    for kcc in range(4)]
                B.grp("tensor", fns, reads=ws[4:8] + aobf, writes=[bank])
                B.dma("sync", "qs", A[7][:, 0:512], x_in.ap()[dmc, :, s0:s0 + 512],
                      writes=[A[7]])
                B.op("vector", lambda h, bank=bank, bsl=bsl, dmc=dmc: h.tensor_scalar(
                    oz[:, :], bank.h[:, bsl],
                    svecs[:, VB_OUT + dmc:VB_OUT + dmc + 1], None, ALU.add),
                    reads=[bank, svecs], writes=[oz])
                B.op("vector", lambda h, dmc=dmc: h.tensor_tensor(
                    ot1[:, :], oz[:, 0:512], Cp[dmc][:, 0:512], ALU.mult),
                    reads=[oz, Cp[dmc]], writes=[ot1])
                B.op("vector", lambda h: h.tensor_tensor(
                    og1[:, :], A[7][:, 0:512], ot1[:, 0:512], ALU.add),
                    reads=[A[7], ot1], writes=[og1])
                B.dma("sync", "qs", x1_d.ap()[dmc, :, s0:s0 + 512], og1[:, 0:512],
                      reads=[og1], writes=[dt("x1")])

    if stop == "tg1":
        for qq in range(4):
            B.barrier("sync", "qs" + str(qq))
        B.emit()
        return nc
    if do_l1:
        for tt in range(4):
            s0 = tt * 512
            for i in range(8):
                B.dma("sync", "qs", A[i][:, 0:512], x1_d.ap()[i, :, s0:s0 + 512],
                      reads=[dt("x1")], writes=[A[i]])
            xts = [(A[i], SL512) for i in range(8)]
            sqs = [(Cp[i], SL512) for i in range(8)]
            stats_tile(xts, 512, sqs, with_rsqv=True)
            bcast2(r_rstd, r_mrstd, 512, PS[2])
            outs = [(xnbf[i], (slice(0, 128), slice(s0, s0 + 512))) for i in range(8)]
            ln_apply(xts, 512, outs, PS[2], tmpA=oz)
            bcast2(r_rsqv, r_mrsqv, 512, PS[2])
            outs2 = [(Dp[2 * i + tt // 2],
                      (slice(0, 128), slice((tt % 2) * 512, (tt % 2) * 512 + 512)))
                     for i in range(8)]
            ln_apply(xts, 512, outs2, PS[2], gamma_base=VGAM1, tmpA=oz, tmpB=ou0)

        def xnn_sl(kc, a, b):
            half = a // 1024
            return Dp[2 * kc + half], (slice(0, 128), slice(a - half * 1024, b - half * 1024))

        def qkva_mm(wt, bank, bsl, a, b):
            fns = []
            rds = [wt]
            for kcc in range(8):
                t, sl = xnn_sl(kcc, a, b)
                rds.append(t)
                fns.append(lambda h, t=t, sl=sl, kcc=kcc: h.matmul(
                    bank.h[:, bsl], wt[:, kcc * 128:(kcc + 1) * 128], t.h[sl],
                    start=(kcc == 0), stop=(kcc == 7)))
            B.grp("tensor", fns, reads=rds, writes=[bank])

        def scan_phase(ccn, full):
            wt = wg[0]
            B.wload(wt, qkva_g.ap()[8 + ccn].rearrange("k p j -> p k j"), dst_ap=wt.h[:, :].rearrange("p (k j) -> p k j", k=8))
            wt2 = wg[1]
            B.wload(wt2, qkva_g.ap()[16 + ccn].rearrange("k p j -> p k j"), dst_ap=wt2.h[:, :].rearrange("p (k j) -> p k j", k=8))
            for tt in range(4):
                bsl = slice((tt % 2) * 512, (tt % 2) * 512 + 512)
                asl = slice((tt % 2) * 512, (tt % 2) * 512 + 512)
                qkva_mm(wt, PS[0], bsl, tt * 512, (tt + 1) * 512)
                B.op("scalar", lambda h, bsl=bsl, tt=tt, asl=asl: h.activation(
                    A[tt // 2].h[:, asl], PS[0].h[:, bsl], AF.Copy),
                    reads=[PS[0]], writes=[A[tt // 2]])
                qkva_mm(wt2, PS[1], bsl, tt * 512, (tt + 1) * 512)
                B.op("scalar", lambda h, bsl=bsl, tt=tt, asl=asl: h.activation(
                    A[2 + tt // 2].h[:, asl], PS[1].h[:, bsl], AF.Sigmoid),
                    reads=[PS[1]], writes=[A[2 + tt // 2]])
            B.op("vector", lambda h: h.tensor_tensor_scan(
                A[4][:, :], A[2][:, :], A[0][:, :], 0.0, ALU.mult, ALU.add),
                reads=[A[2], A[0]], writes=[A[4]])
            B.op("vector", lambda h: h.tensor_tensor_scan(
                A[5][:, :], A[3][:, :], A[1][:, :], A[4][:, 1023:1024],
                ALU.mult, ALU.add),
                reads=[A[3], A[1], A[4]], writes=[A[5]])
            if full:
                B.op("vector", lambda h: h.tensor_tensor_scan(
                    A[6][:, :], A[2][:, :], A[2][:, :], 1.0, ALU.mult, ALU.bypass),
                    reads=[A[2]], writes=[A[6]])
                B.op("vector", lambda h: h.tensor_tensor_scan(
                    A[7][:, :], A[3][:, :], A[3][:, :], A[6][:, 1023:1024],
                    ALU.mult, ALU.bypass),
                    reads=[A[3], A[6]], writes=[A[7]])

        for ccn in range(8):
            scan_phase(ccn, full=False)
            B.op("vector", lambda h, ccn=ccn: h.tensor_copy(
                carry_sb[:, ccn:ccn + 1], A[5][:, 1023:1024]),
                reads=[A[5]], writes=[carry_sb])
        for ccn in range(8):
            B.dma("gpsimd", "w3",
                  carry_in.ap().rearrange("o (c p) -> (o c) p", p=128)[ccn]
                  .rearrange("p -> p ()"),
                  carry_sb[:, ccn:ccn + 1], reads=[carry_sb], writes=[dt("cc_in")])
        B.op("gpsimd", lambda h: h.collective_compute(
            "AllGather", ALU.bypass,
            replica_groups=[[0, 1], [2, 3], [4, 5], [6, 7]],
            ins=[carry_in.ap().opt()], outs=[carry_out.ap().opt()]),
            reads=[dt("cc_in")], writes=[dt("cc_out")], chkey="cc")
        B.barrier("gpsimd", "cc")
        for ccn in range(8):
            B.dma("gpsimd", "w3", carryin_sb[:, ccn:ccn + 1],
                  carry_out.ap()[0:1].rearrange("o (c p) -> (o c) p", p=128)[ccn]
                  .rearrange("p -> p ()"),
                  reads=[dt("cc_out")], writes=[carryin_sb])
        B.op("vector", lambda h: h.tensor_scalar(
            carryin_sb[:, :], carryin_sb[:, :], role_sb[:, 0:1], None, ALU.mult),
            reads=[carryin_sb, role_sb], writes=[carryin_sb])

        for ccn in range(8):
            scan_phase(ccn, full=True)
            B.op("vector", lambda h, ccn=ccn: h.scalar_tensor_tensor(
                A[4][:, :], A[6][:, :], carryin_sb[:, ccn:ccn + 1], A[4][:, :],
                ALU.mult, ALU.add),
                reads=[A[6], carryin_sb], writes=[A[4]])
            B.op("vector", lambda h, ccn=ccn: h.scalar_tensor_tensor(
                A[5][:, :], A[7][:, :], carryin_sb[:, ccn:ccn + 1], A[5][:, :],
                ALU.mult, ALU.add),
                reads=[A[7], carryin_sb], writes=[A[5]])
            wt3 = wg[2]
            B.wload(wt3, qkva_g.ap()[ccn].rearrange("k p j -> p k j"), dst_ap=wt3.h[:, :].rearrange("p (k j) -> p k j", k=8))
            for tt in range(4):
                bsl = slice((tt % 2) * 512, (tt % 2) * 512 + 512)
                qkva_mm(wt3, PS[2], bsl, tt * 512, (tt + 1) * 512)
                yt = A[4 + tt // 2]
                ysl = slice((tt % 2) * 512, (tt % 2) * 512 + 512)
                B.op("vector", lambda h, bsl=bsl, yt=yt, ysl=ysl: h.tensor_tensor(
                    yt.h[:, ysl], PS[2].h[:, bsl], yt.h[:, ysl], ALU.mult),
                    reads=[PS[2], yt], writes=[yt])
            B.dma("sync", "qs", ssp.ap()[ccn, :, 0:1024], A[4][:, :],
                  reads=[A[4]], writes=[dt("ssp")])
            B.dma("sync", "qs", ssp.ap()[ccn, :, 1024:2048], A[5][:, :],
                  reads=[A[5]], writes=[dt("ssp")])

        for k in range(8):
            B.wload(ws[k], shared_g[k])
        for tg in range(2):
            t0 = tg * 1024
            for tq in range(8):
                tc = tg * 8 + tq
                gbank = PS[tq % 2]
                fns = [lambda h, kcc=kcc, tc=tc, gbank=gbank: h.matmul(
                    gbank.h[:, 0:8], xnbf[kcc][:, tc * 128:(tc + 1) * 128],
                    gx[:, kcc * 8:(kcc + 1) * 8], start=(kcc == 0), stop=(kcc == 7))
                    for kcc in range(8)]
                B.grp("tensor", fns, reads=xnbf + [gx], writes=[gbank])
                B.op("scalar", lambda h, gbank=gbank: h.activation(
                    p8[:, :], gbank.h[:, 0:8], AF.Exp, bias=cc(C_G8)),
                    reads=[gbank, sconst], writes=[p8])
                B.op("vector", lambda h: h.tensor_reduce(
                    m1c[:, :], p8[:, :], axis=AX.X, op=ALU.add),
                    reads=[p8], writes=[m1c])
                B.op("vector", lambda h: h.reciprocal(m1c[:, :], m1c[:, :]),
                     reads=[m1c], writes=[m1c])
                B.op("vector", lambda h: h.tensor_scalar(
                    p8[:, :], p8[:, :], m1c[:, 0:1], None, ALU.mult),
                    reads=[p8, m1c], writes=[p8])
                B.op("vector", lambda h: h.tensor_reduce(
                    m1c[:, :], p8[:, :], axis=AX.X, op=ALU.max),
                    reads=[p8], writes=[m1c])
                B.op("vector", lambda h: h.tensor_scalar(
                    p8b[:, :], p8[:, :], m1c[:, 0:1], -2.0, ALU.is_ge, ALU.mult),
                    reads=[p8, m1c], writes=[p8b])
                B.op("vector", lambda h: h.tensor_tensor(
                    p8b[:, :], p8[:, :], p8b[:, :], ALU.add),
                    reads=[p8, p8b], writes=[p8b])
                B.op("vector", lambda h: h.tensor_reduce(
                    m2c[:, :], p8b[:, :], axis=AX.X, op=ALU.max),
                    reads=[p8b], writes=[m2c])
                B.op("vector", lambda h: h.tensor_scalar(
                    p8b[:, :], p8[:, :], m2c[:, 0:1], None, ALU.is_ge),
                    reads=[p8, m2c], writes=[p8b])
                B.op("vector", lambda h: h.tensor_tensor(
                    p8b[:, :], p8b[:, :], p8[:, :], ALU.mult),
                    reads=[p8b, p8], writes=[p8b])
                B.op("vector", lambda h: h.tensor_reduce(
                    m2c[:, :], p8b[:, :], axis=AX.X, op=ALU.add),
                    reads=[p8b], writes=[m2c])
                B.op("vector", lambda h: h.reciprocal(m2c[:, :], m2c[:, :]),
                     reads=[m2c], writes=[m2c])
                B.op("vector", lambda h, tq=tq: h.tensor_scalar(
                    comb[tq][:, :], p8b[:, :], m2c[:, 0:1], None, ALU.mult),
                    reads=[p8b, m2c], writes=[comb[tq]])
            for tq in range(8):
                B.grp("tensor", [lambda h, tq=tq: h.transpose(
                    PS[3].h[0:8, 0:128], comb[tq][:, :], sident_f[:, :])],
                    reads=[comb[tq], sident_f], writes=[PS[3]])
                B.op("vector", lambda h, tq=tq: h.tensor_copy(
                    cbstage[:, tq * 128:(tq + 1) * 128], PS[3][0:8, 0:128]),
                    reads=[PS[3]], writes=[cbstage])
            B.dma("sync", "qs", cb_d[:, :], cbstage[:, :],
                  reads=[cbstage], writes=[dt("cb_d")])

            for hf in range(2):
                th0 = t0 + hf * 512
                for g in range(2):
                    for dq in range(4):
                        dmc = g * 4 + dq
                        bank = PS[dq]
                        bsl = slice(hf * 512, hf * 512 + 512)
                        fns = [lambda h, kcc=kcc, dmc=dmc, bank=bank, bsl=bsl,
                               th0=th0: h.matmul(
                            bank.h[:, bsl], ws[kcc][:, dmc * 128:(dmc + 1) * 128],
                            xnbf[kcc][:, th0:th0 + 512],
                            start=(kcc == 0), stop=(kcc == 7)) for kcc in range(8)]
                        B.grp("tensor", fns, reads=ws[:8] + xnbf, writes=[bank])
                        B.op("scalar", lambda h, bank=bank, bsl=bsl, dmc=dmc: h.activation(
                            gf[:, 0:512], bank.h[:, bsl], AF.Gelu,
                            bias=svecs[:, VSHB + dmc:VSHB + dmc + 1]),
                            reads=[bank, svecs], writes=[gf])
                        B.op("vector", lambda h, dmc=dmc, hf=hf: h.tensor_scalar(
                            Cp[dmc][:, hf * 512:hf * 512 + 512], gf[:, 0:512], 0.5,
                            None, ALU.mult),
                            reads=[gf], writes=[Cp[dmc]])

            for e in range(NE):
                B.dma("gpsimd", "w3", cbrow_bf[0:1, :], cb_d[e:e + 1, :],
                      reads=[dt("cb_d")], writes=[cbrow_bf])
                for hf2 in range(2):
                    B.grp("tensor", [lambda h, hf2=hf2: h.matmul(
                        PS[3].h[:, hf2 * 512:hf2 * 512 + 512], onesrow_bf[0:1, 0:128],
                        cbrow_bf[0:1, hf2 * 512:hf2 * 512 + 512],
                        start=True, stop=True)],
                        reads=[onesrow_bf, cbrow_bf], writes=[PS[3]])
                B.op("vector", lambda h: h.tensor_copy(cbb[:, :], PS[3][:, :]),
                     reads=[PS[3]], writes=[cbb])
                for half in range(2):
                    for hcq in range(16):
                        hc = half * 16 + hcq
                        wt = wg[hcq % 4]
                        abank = PS[hcq % 2]
                        B.wload(wt, w1e_g.ap()[e, hc].rearrange("k p j -> p k j"), dst_ap=wt.h[:, :].rearrange("p (k j) -> p k j", k=8))
                        for hf in range(2):
                            bsl = slice(hf * 512, hf * 512 + 512)
                            fns = [lambda h, kcc=kcc, wt=wt, bsl=bsl, hf=hf, t0=t0,
                                   abank=abank: h.matmul(
                                abank.h[:, bsl], wt[:, kcc * 128:(kcc + 1) * 128],
                                xnbf[kcc][:, t0 + hf * 512:t0 + hf * 512 + 512],
                                start=(kcc == 0), stop=(kcc == 7)) for kcc in range(8)]
                            B.grp("tensor", fns, reads=[wt] + xnbf, writes=[abank])
                        B.op("scalar", lambda h, hcq=hcq, hc=hc, e=e, abank=abank: h.activation(
                            Dp[hcq][:, :], abank[:, :], AF.Gelu,
                            bias=b1e_s[:, e * 32 + hc:e * 32 + hc + 1]),
                            reads=[abank, b1e_s], writes=[Dp[hcq]])
                    for g in range(2):
                        for hcq in range(16):
                            hc = half * 16 + hcq
                            wt = wg[(hcq + g * 2) % 4]
                            B.wload(wt, w2e_g.ap()[e, hc])
                            fns = []
                            for dq in range(4):
                                dmc = g * 4 + dq
                                for hf in range(2):
                                    fns.append(lambda h, wt=wt, dmc=dmc, dq=dq, hf=hf,
                                               hcq=hcq: h.matmul(
                                        PS[dq].h[:, hf * 512:hf * 512 + 512],
                                        wt[:, dmc * 128:(dmc + 1) * 128],
                                        Dp[hcq][:, hf * 512:hf * 512 + 512],
                                        start=(hcq == 0), stop=(hcq == 15)))
                            B.grp("tensor", fns,
                                  reads=[wt] + [Dp[q] for q in range(16)],
                                  writes=[PS[0], PS[1], PS[2], PS[3]])
                        for dq in range(4):
                            dmc = g * 4 + dq
                            B.op("vector", lambda h, dq=dq, dmc=dmc, e=e: h.scalar_tensor_tensor(
                                gf2[:, :], PS[dq].h[:, :],
                                b2c_s[:, e * 8 + dmc:e * 8 + dmc + 1],
                                cbb[:, :], ALU.add, ALU.mult),
                                reads=[PS[dq], b2c_s, cbb], writes=[gf2])
                            B.op("vector", lambda h, dmc=dmc: h.tensor_tensor(
                                Cp[dmc][:, :], Cp[dmc][:, :], gf2[:, :], ALU.add),
                                reads=[gf2], writes=[Cp[dmc]])

            for sub in range(2):
                u0 = sub * 512
                xts = [(Cp[i], (slice(0, 128), slice(u0, u0 + 512))) for i in range(8)]
                sqs = [(A[i // 2], (slice(0, 128), slice((i % 2) * 512, (i % 2) * 512 + 512)))
                       for i in range(8)]
                stats_tile(xts, 512, sqs)
                bcast2(r_rstd, r_mrstd, 512, PS[0])
                for ci in range(8):
                    B.op("vector", lambda h, ci=ci, u0=u0: h.tensor_tensor(
                        gf[:, 0:512], Cp[ci][:, u0:u0 + 512], PS[0][:, 0:512], ALU.mult),
                        reads=[Cp[ci], PS[0]], writes=[gf])
                    B.op("vector", lambda h: h.tensor_tensor(
                        gf[:, 0:512], gf[:, 0:512], PS[0][:, 512:1024], ALU.subtract),
                        reads=[gf, PS[0]], writes=[gf])
                    B.op("vector", lambda h, ci=ci: h.tensor_scalar(
                        gf[:, 0:512], gf[:, 0:512],
                        svecs[:, VGAMF + ci:VGAMF + ci + 1],
                        svecs[:, VBETF + ci:VBETF + ci + 1], ALU.mult, ALU.add),
                        reads=[gf, svecs], writes=[gf])
                    B.op("scalar", lambda h: h.activation(
                        gf2[:, 0:512], gf[:, 0:512], AF.Silu),
                        reads=[gf], writes=[gf2])
                    B.dma("sync", "qs", A[6][:, 0:512],
                          x1_d.ap()[ci, :, t0 + u0:t0 + u0 + 512],
                          reads=[dt("x1")], writes=[A[6]])
                    B.dma("sync", "qs", A[7][:, 0:512],
                          ssp.ap()[ci, :, t0 + u0:t0 + u0 + 512],
                          reads=[dt("ssp")], writes=[A[7]])
                    B.op("vector", lambda h: h.tensor_tensor(
                        gf2[:, 0:512], A[7][:, 0:512], gf2[:, 0:512], ALU.mult),
                        reads=[A[7], gf2], writes=[gf2])
                    B.op("vector", lambda h: h.tensor_tensor(
                        gf2[:, 0:512], A[6][:, 0:512], gf2[:, 0:512], ALU.add),
                        reads=[A[6], gf2], writes=[gf2])
                    B.dma("sync", "qs", out_d.ap()[ci, :, t0 + u0:t0 + u0 + 512],
                          gf2[:, 0:512], reads=[gf2])
    if do_l1 or part == "l0":
        for qq in range(4):
            B.barrier("sync", "qs" + str(qq))
    B.emit()
    return nc



def _shared_inputs(inp):
    f = np.float32
    asc = np.ascontiguousarray
    d = {}
    w_qkv = inp["L0_attn_w_qkv"]
    wqT = asc(w_qkv[:512].T)
    d["wq_g"] = asc(wqT.reshape(8, 128, 4, 128).transpose(2, 0, 1, 3))
    d["wkvT"] = asc(w_qkv[512:1536].T.reshape(8, 128, 1024))
    d["woutT_g"] = asc(inp["L0_attn_w_out"].T.reshape(4, 128, 1024))
    w1 = inp["L0_ffn_w1"]
    d["w1_g"] = asc(w1.reshape(8, 128, 32, 128).transpose(2, 0, 1, 3))
    d["w2_g"] = asc(inp["L0_ffn_w2"].reshape(32, 128, 1024))
    d["b1l0"] = asc(inp["L0_ffn_b1"].reshape(32, 128).T)
    d["qkva_g"] = asc(inp["L1_attn_W_qkva"].T.reshape(8, 128, 24, 128)
                      .transpose(2, 0, 1, 3))
    d["gate_g"] = asc(inp["L1_ffn_gate_W"].reshape(8, 128, 8))
    d["shared_g"] = asc(inp["L1_ffn_shared_W"].reshape(8, 128, 1024))
    d["w1e_g"] = asc(inp["L1_ffn_W1"].reshape(8, 8, 128, 32, 128)
                     .transpose(0, 3, 1, 2, 4))
    d["w2e_g"] = asc(inp["L1_ffn_W2"].reshape(8, 32, 128, 1024))
    d["b1e"] = asc(inp["L1_ffn_b1"].reshape(8, 32, 128).transpose(2, 0, 1)
                   .reshape(128, 256))
    d["b2c"] = asc(inp["L1_ffn_b2"].reshape(8, 8, 128).transpose(2, 0, 1)
                   .reshape(128, 64))
    vecs = np.zeros((128, 64), f)
    vecs[:, VB_OUT:VB_OUT + 8] = inp["L0_attn_b_out"].reshape(8, 128).T
    vecs[:, VB2L0:VB2L0 + 8] = inp["L0_ffn_b2"].reshape(8, 128).T
    vecs[:, VGAM1:VGAM1 + 8] = inp["L1_attn_gamma"].reshape(8, 128).T
    vecs[:, VSHB:VSHB + 8] = inp["L1_ffn_shared_b"].reshape(8, 128).T
    vecs[:, VGAMF:VGAMF + 8] = inp["L1_ffn_gamma"].reshape(8, 128).T
    vecs[:, VBETF:VBETF + 8] = inp["L1_ffn_beta"].reshape(8, 128).T
    d["vecs"] = vecs
    cvals = _const_layout()[0]
    NCON = max(64, len(cvals))
    consts = np.zeros((128, NCON), f)
    for i, v in enumerate(cvals):
        consts[:, i] = v
    d["consts"] = consts
    d["ident"] = np.eye(128, dtype=f)
    return {k: np.asarray(v, f) for k, v in d.items()}


_L0_KEYS = ("xown", "xoth", "wq_g", "wkvT", "woutT_g", "w1_g", "w2_g", "b1l0",
            "vecs", "consts", "ident")
_L1_KEYS = ("qkva_g", "gate_g", "shared_g", "w1e_g", "w2e_g", "b1e", "b2c",
            "role", "vecs", "consts", "ident", "x1")


def kernel(**inp):
    inp = {k: np.asarray(v) for k, v in inp.items()}
    shared = _shared_inputs(inp)
    x = np.asarray(inp["L0_attn_x"], np.float32)
    cores = []
    for c in range(8):
        b, hh = c // 2, c % 2
        xo = np.ascontiguousarray(x[b, hh * 2048:(hh + 1) * 2048, :].T
                                  .reshape(8, 128, T))
        xt = np.ascontiguousarray(x[b, (1 - hh) * 2048:(2 - hh) * 2048, :].T
                                  .reshape(8, 128, T))
        role = np.full((128, 1), float(hh), np.float32)
        cores.append(dict(xown=xo, xoth=xt, role=role))

    if PART == "full":
        nc = build_program("full")
        in_maps = []
        for c in range(8):
            m = dict(shared)
            m.update({k: v for k, v in cores[c].items()})
            in_maps.append(m)
        res = run_bass_kernel_spmd(nc, in_maps, core_ids=list(range(8)))
        outs = [r["out"] for r in res.results]
    else:
        nc0 = build_program("l0")
        in_maps = []
        for c in range(8):
            m = {k: shared[k] for k in _L0_KEYS if k in shared}
            m["xown"] = cores[c]["xown"]
            m["xoth"] = cores[c]["xoth"]
            in_maps.append(m)
        res0 = run_bass_kernel_spmd(nc0, in_maps, core_ids=list(range(8)))
        x1s = [np.asarray(r["x1"], np.float32) for r in res0.results]
        nc1 = build_program("l1")
        in_maps = []
        for c in range(8):
            m = {k: shared[k] for k in _L1_KEYS if k in shared}
            m["role"] = cores[c]["role"]
            m["x1"] = x1s[c]
            in_maps.append(m)
        res = run_bass_kernel_spmd(nc1, in_maps, core_ids=list(range(8)))
        outs = [r["out"] for r in res.results]

    full = np.empty((4, 4096, 1024), np.float32)
    for c in range(8):
        b, hh = c // 2, c % 2
        o = np.asarray(outs[c], np.float32).reshape(1024, T)
        full[b, hh * 2048:(hh + 1) * 2048, :] = o.T
    return full


if __name__ == "__main__":
    import sys
    part = sys.argv[1] if len(sys.argv) > 1 else "l0"
    nc = build_program(part)
    print("built", part)


# version 17
# speedup vs baseline: 1.2287x; 1.2287x over previous
import numpy as np
import ml_dtypes
import concourse.bass as bass
import concourse.mybir as mybir
from concourse.bass_utils import run_bass_kernel_spmd

F32 = mybir.dt.float32
BF16 = mybir.dt.bfloat16
I32 = mybir.dt.int32
AF = mybir.ActivationFunctionType
ALU = mybir.AluOpType
AX = mybir.AxisListType

T = 2048
D = 1024
CH = 8
NE = 8
PART = "full"

_XBITS = [3229614080, 3228565504, 3227516928, 3226468352, 3225419776, 3224371200,
          3223322624, 3222274048, 3221225472, 3219128320, 3217031168, 3214934016,
          3212836864, 3208642560, 3204448256, 3196059648, 0, 1048576000, 1056964608,
          1061158912, 1065353216, 1067450368, 1069547520, 1071644672, 1073741824,
          1074790400, 1075838976, 1076887552, 1077936128, 1078984704, 1080033280,
          1081081856, 1082130432]
_YBITS = [3104096256, 3115170560, 3126160128, 3136671360, 3146036032, 3154526592,
          3162396960, 3168883268, 3174719104, 3180302958, 3184343880, 3188144980,
          3189929606, 3190688958, 3189635234, 3184359005, 0, 1041843410, 1051788207,
          1058307280, 1062691422, 1066342549, 1068706892, 1071056601, 1073360140,
          1074675036, 1075773863, 1076853182, 1077919142, 1078976838, 1080029866,
          1081080467, 1082129901]


def _pwl_terms():
    X = np.array(_XBITS, np.uint32).view(np.float32).astype(np.float64)
    Y = np.array(_YBITS, np.uint32).view(np.float32).astype(np.float64)
    s = (Y[1:] - Y[:-1]) / 0.25
    c = np.empty(33)
    c[0] = s[0]
    c[1:32] = s[1:] - s[:-1]
    c[32] = 1.0 - s[31]
    U = np.concatenate([X[:32] * 256.0, [1024.0]])
    terms = [(float(c[i]), float(U[i]), +1.0) for i in range(33)]
    d = 256.0 * Y[0]
    terms += [(-d, 1024.0, +1.0), (d, 1025.0, +1.0),
              (-d, 1024.0, -1.0), (d, 1025.0, -1.0)]
    const = 256.0 * Y[0] + 0.5
    return terms, const

PWL_TERMS, PWL_CONST = _pwl_terms()

def _pwl_terms_ge():
    X = np.array(_XBITS, np.uint32).view(np.float32).astype(np.float64)
    Y = np.array(_YBITS, np.uint32).view(np.float32).astype(np.float64)
    t = X[16:]
    G = Y[16:] - t / 2.0
    sG = (G[1:] - G[:-1]) / 0.25
    cG = np.empty(17)
    cG[0] = sG[0]
    cG[1:16] = sG[1:] - sG[:-1]
    cG[16] = 0.5 - sG[15]
    UG = np.concatenate([t[:16] * 256.0, [1024.0]])
    dstep = 256.0 * (4.0 - Y[32])
    Ev = np.array([Y[j] - (X[j] / 2.0 + G[16 - j]) for j in range(9)])
    d = np.zeros(9)
    for i in range(7, -1, -1):
        s = sum(d[j] * (X[j] - X[i]) for j in range(i + 1, 9))
        d[i + 1] = (Ev[i] - s) / (X[i + 1] - X[i])
    terms = []
    for i in range(17):
        terms.append(("a", float(cG[i]), float(UG[i])))
    terms.append(("a", float(dstep), 1024.0))
    terms.append(("a", float(-dstep), 1025.0))
    for j in range(1, 9):
        terms.append(("u", float(256.0 * d[j]), float(-256.0 * X[j])))
    cjs = [(c, U) for (s, c, U) in terms if s == "u"]
    def Elr(u):
        return sum(c * max(-u - U, 0.0) for c, U in cjs)
    v1025 = Elr(-1025.0)
    v1026 = Elr(-1026.0)
    m = v1026 - v1025
    p = -v1025
    q = -v1026 - 2.0 * p
    terms.append(("u", float(p), 1024.0))
    terms.append(("u", float(q), 1025.0))
    def ge_y(u):
        acc = 0.5 + 0.5 * u
        for (s, c, U) in terms:
            acc += c * max((abs(u) if s == "a" else -u) - U, 0.0)
        return acc
    tref, cref = _pwl_terms()
    yref = cref + sum(c * max(sd * -740.0 - U, 0.0) for (c, U, sd) in tref)
    dfix = yref - ge_y(-740.0)
    return terms, float(dfix)

PWL_GE, PWL_DFIX = _pwl_terms_ge()
PWL_POOL = frozenset()



class Chan:
    def __init__(self, sem, inc):
        self.sem = sem
        self.inc = inc
        self.count = 0


class Tile:
    __slots__ = ("h", "name", "wr", "rd")
    def __init__(self, h, name):
        self.h = h
        self.name = name
        self.wr = None
        self.rd = {}
    def __getitem__(self, idx):
        return self.h[idx]


class Bld:
    ENGS = ("tensor", "vector", "scalar", "gpsimd", "sync")

    def __init__(self, nc):
        self.nc = nc
        self._cm = []
        self.q = {e: [] for e in self.ENGS}
        self.ch = {}
        for name, inc in [("t", 1), ("v", 1), ("a", 1), ("p", 1),
                          ("qs0", 16), ("qs1", 16), ("qs2", 16), ("qs3", 16),
                          ("w0", 16), ("w1", 16), ("w2", 16), ("w3", 16), ("cc", 1)]:
            cm = nc.semaphore("ch_" + name)
            sem = cm.__enter__()
            self._cm.append(cm)
            self.ch[name] = Chan(sem, inc)
        self.clock = {}
        self._eng2ch = {"tensor": "t", "vector": "v", "scalar": "a", "gpsimd": "p"}
        self._wrot = 0
        self._qrot = 0
        self.nins = 0

    def sb(self, name, p, f, dt):
        return Tile(self.nc.alloc_sbuf_tensor(name, [p, f], dt), name)

    def ps(self, name, p, f, dt=F32):
        return Tile(self.nc.alloc_psum_tensor(name, [p, f], dt), name)

    def _wait(self, eng, chkey, count):
        if count <= 0:
            return
        if eng == "tensor" and chkey == "t":
            return
        key = (eng, chkey)
        if self.clock.get(key, 0) >= count:
            return
        self.clock[key] = count
        sem = self.ch[chkey].sem
        self.q[eng].append(lambda h, s=sem, c=count: h.wait_ge(s, c))

    def _pre(self, eng, reads, writes):
        for t in reads:
            if t.wr is not None:
                self._wait(eng, t.wr[0], t.wr[1])
        for t in writes:
            if t.wr is not None:
                self._wait(eng, t.wr[0], t.wr[1])
            for ck, cnt in t.rd.items():
                self._wait(eng, ck, cnt)

    def _post(self, chkey, reads, writes):
        ch = self.ch[chkey]
        ch.count += ch.inc
        for t in reads:
            t.rd[chkey] = ch.count
        for t in writes:
            t.wr = (chkey, ch.count)
            t.rd = {}

    def op(self, eng, fn, reads=(), writes=(), chkey=None):
        ck = chkey or self._eng2ch[eng]
        self._pre(eng, list(reads), list(writes))
        self._post(ck, list(reads), list(writes))
        sem, icr = self.ch[ck].sem, self.ch[ck].inc
        self.q[eng].append(lambda h, f=fn, s=sem, c=icr: f(h).then_inc(s, c))
        self.nins += 1

    def grp(self, eng, fns, reads=(), writes=(), chkey=None):
        ck = chkey or self._eng2ch[eng]
        self._pre(eng, list(reads), list(writes))
        self._post(ck, list(reads), list(writes))
        sem, icr = self.ch[ck].sem, self.ch[ck].inc
        for f in fns[:-1]:
            self.q[eng].append(f)
        self.q[eng].append(lambda h, f=fns[-1], s=sem, c=icr: f(h).then_inc(s, c))
        self.nins += len(fns)

    def dma(self, qeng, chkey, dst_ap, src_ap, reads=(), writes=()):
        if chkey == "qs":
            chkey = "qs" + str(self._qrot % 4)
            self._qrot += 1
        self._pre(qeng, list(reads), list(writes))
        self._wait(qeng, chkey, self.ch[chkey].count)
        self._post(chkey, list(reads), list(writes))
        sem = self.ch[chkey].sem
        self.q[qeng].append(
            lambda h, o=dst_ap, i=src_ap, s=sem: h.dma_start(o, i).then_inc(s, 16))
        self.nins += 1

    def wload(self, dst_tile, src_ap, dst_ap=None):
        ck = "w" + str(self._wrot % 4)
        self._wrot += 1
        self.dma("gpsimd", ck,
                 dst_ap if dst_ap is not None else dst_tile[:, :], src_ap,
                 writes=[dst_tile])

    def barrier(self, eng, chkey):
        self._wait(eng, chkey, self.ch[chkey].count)

    def emit(self):
        with self.nc.Block() as block:
            for e in self.ENGS:
                if not self.q[e]:
                    continue
                dec = getattr(block, e)
                def section(h, q=tuple(self.q[e])):
                    for f in q:
                        f(h)
                dec(section)


def _const_layout():
    cvals = []
    def C(v):
        cvals.append(float(v))
        return len(cvals) - 1
    C_EXPS = C(-20.0)
    C_G8 = C(-8.0)
    C_INV256 = C(2.0 ** -8)
    C_EPS = C(1e-5)
    C_256 = C(256.0)
    pwl_cols = []
    for (srt, c, U) in PWL_GE:
        pwl_cols.append((C(abs(c)), C(-abs(c) * U), 1.0 if c > 0 else -1.0))
    return cvals, C_EXPS, C_G8, C_INV256, C_EPS, C_256, pwl_cols


VB_OUT = 0
VB2L0 = 8
VGAM1 = 16
VSHB = 24
VGAMF = 32
VBETF = 40


def build_program(part, stop=None):
    nc = bass.Bass("TRN2", target_bir_lowering=False)
    B = Bld(nc)
    do_l0 = part in ("l0", "full")
    do_l1 = part in ("l1", "full")

    cvals, C_EXPS, C_G8, C_INV256, C_EPS, C_256, pwl_cols = _const_layout()
    NCON = max(64, len(cvals))

    din = lambda n, s, dt=F32: nc.dram_tensor(n, list(s), dt, kind="ExternalInput")
    dout = lambda n, s, dt=F32: nc.dram_tensor(n, list(s), dt, kind="ExternalOutput")
    dint = lambda n, s, dt=F32: nc.dram_tensor(n, list(s), dt, kind="Internal")

    if do_l0:
        x_in = din("xown", (CH, 128, T))
        xoth = din("xoth", (CH, 128, T))
        wq_g = din("wq_g", (4, 8, 128, 128))
        wkvT = din("wkvT", (8, 128, 1024))
        woutT_g = din("woutT_g", (4, 128, 1024))
        w1_g = din("w1_g", (32, 8, 128, 128))
        w2_g = din("w2_g", (32, 128, 1024))
        b1l0_in = din("b1l0", (128, 32))
    if do_l1:
        qkva_g = din("qkva_g", (24, 8, 128, 128))
        gate_g = din("gate_g", (8, 128, 8))
        shared_g = din("shared_g", (8, 128, 1024))
        w1e_g = din("w1e_g", (NE, 32, 8, 128, 128))
        w2e_g = din("w2e_g", (NE, 32, 128, 1024))
        b1e_in = din("b1e", (128, 256))
        b2c_in = din("b2c", (128, 64))
        role_in = din("role", (128, 1))
        carry_in = dint("cc_in", (1, 1024))
        carry_out = dint("cc_out", (2, 1024))
        cb_d = dint("cb_d", (8, 1024))
        ssp = dint("sspill", (CH, 128, T))
        out_d = dout("out", (CH, 128, T))
    vecs_in = din("vecs", (128, 64))
    consts_in = din("consts", (128, NCON))
    ident_in = din("ident", (128, 128))
    if part == "l0":
        x1_d = dout("x1", (CH, 128, T))
    elif part == "l1":
        x1_d = din("x1", (CH, 128, T))
    else:
        x1_d = dint("x1", (CH, 128, T))

    sb, ps = B.sb, B.ps
    A = [sb(f"A{i}", 128, 1024, F32) for i in range(8)]
    Cp = [sb(f"C{i}", 128, 1024, F32) for i in range(8)]
    Dp = [sb(f"D{i}", 128, 1024, BF16) for i in range(16)]
    xnbf = [sb(f"xn{i}", 128, T, BF16) for i in range(CH)]
    wg = [sb(f"wg{i}", 128, 1024, BF16) for i in range(4)]
    ws = [sb(f"ws{i}", 128, 1024, BF16) for i in range(8)]
    r_m = sb("r_m", 1, 512, F32)
    r_v = sb("r_v", 1, 512, F32)
    r_rstd = sb("r_rstd", 1, 512, F32)
    r_mrstd = sb("r_mrstd", 1, 512, F32)
    r_rsqv = sb("r_rsqv", 1, 512, F32)
    r_mrsqv = sb("r_mrsqv", 1, 512, F32)
    r_t0 = sb("r_t0", 1, 512, F32)
    onesrow_f = sb("onesrow_f", 1, 128, F32)
    onesrow_bf = sb("onesrow_bf", 1, 512, BF16)
    onescol_f = sb("onescol_f", 128, 1, F32)
    svecs = sb("svecs", 128, 64, F32)
    sconst = sb("sconst", 128, NCON, F32)
    sident_f = sb("sident_f", 128, 128, F32)
    zti = sb("zti", 128, 1024, I32)
    pacc = Cp[7]
    ozi = sb("ozi", 128, 512, I32)
    oz = sb("oz", 128, 512, F32)
    ou0 = sb("ou0", 128, 512, F32)
    od = sb("od", 128, 512, F32)
    ot1 = sb("ot1", 128, 512, F32)
    oq = sb("oq", 128, 512, F32)
    og1 = sb("og1", 128, 512, F32)
    ek = sb("ek", 128, 512, F32)
    krec = sb("krec", 128, 8, F32)
    kbf = sb("kbf", 128, 512, BF16)
    vbf = sb("vbf", 128, 512, BF16)

    ctx_bd = sb("ctx_bd", 128, 512, BF16)
    qbf = [sb(f"qbf{m}", 128, 512, BF16) for m in range(4)]
    aobf = [sb(f"aobf{p}", 128, 512, BF16) for p in range(4)]
    if do_l0:
        b1cc = sb("b1cc", 128, 32, F32)
        b2cc = sb("b2cc", 128, 8, F32)
    if do_l1:
        b1e_s = sb("b1e_s", 128, 256, F32)
        b2c_s = sb("b2c_s", 128, 64, F32)
        role_sb = sb("role_sb", 128, 1, F32)
        carry_sb = sb("carry_sb", 128, 8, F32)
        carryin_sb = sb("carryin_sb", 128, 8, F32)
        comb = [sb(f"comb{tc}", 128, 8, F32) for tc in range(8)]
        cbstage = sb("cbstage", 8, 1024, F32)
        cbrow_bf = sb("cbrow_bf", 1, 1024, BF16)
        cbb = A[3]
        gx = sb("gx", 128, 64, BF16)
        p8 = sb("p8", 128, 8, F32)
        p8b = sb("p8b", 128, 8, F32)
        m1c = sb("m1c", 128, 1, F32)
        m2c = sb("m2c", 128, 1, F32)
        gf = A[4]
        gf2 = A[5]
    PS = [ps(f"ps{i}", 128, 1024) for i in range(4)]

    def cc(i):
        return sconst[:, i:i + 1]

    class DTile:
        __slots__ = ("name", "wr", "rd")
        def __init__(self, name):
            self.name = name
            self.wr = None
            self.rd = {}
    dT = {}
    def dt(name):
        if name not in dT:
            dT[name] = DTile(name)
        return dT[name]

    B.dma("sync", "qs", svecs[:, :], vecs_in[:, :], writes=[svecs])
    B.dma("sync", "qs", sconst[:, :], consts_in[:, :], writes=[sconst])
    B.dma("sync", "qs", sident_f[:, :], ident_in[:, :], writes=[sident_f])
    B.op("gpsimd", lambda h: h.memset(onesrow_f[:, :], 1.0), writes=[onesrow_f])
    B.op("gpsimd", lambda h: h.memset(onesrow_bf[:, :], 1.0), writes=[onesrow_bf])
    B.op("gpsimd", lambda h: h.memset(onescol_f[:, :], 1.0), writes=[onescol_f])
    if do_l0:
        B.dma("sync", "qs", b1cc[:, :], b1l0_in[:, :], writes=[b1cc])
        B.op("vector", lambda h: h.tensor_scalar(
            b1cc[:, :], b1cc[:, :], 256.0, 0.5, ALU.mult, ALU.add),
            reads=[b1cc], writes=[b1cc])
        B.op("vector", lambda h: h.tensor_scalar(
            b2cc[:, :], svecs[:, VB2L0:VB2L0 + 8], 256.0, 0.5, ALU.mult, ALU.add),
            reads=[svecs], writes=[b2cc])
    if do_l1:
        B.dma("sync", "qs", b1e_s[:, :], b1e_in[:, :], writes=[b1e_s])
        B.dma("sync", "qs", b2c_s[:, :], b2c_in[:, :], writes=[b2c_s])
        B.dma("sync", "qs", role_sb[:, :], role_in[:, :], writes=[role_sb])
        for k in range(8):
            B.dma("gpsimd", "w3", gx[:, k * 8:(k + 1) * 8], gate_g[k], writes=[gx])

    def stats_tile(xts, w, sq_slots, with_rsqv=False):
        tiles = [t for t, _ in xts]
        fns = [lambda h, t=t, sl=sl, i=i: h.matmul(
            PS[3].h[0:1, 0:w], onescol_f[:, :], t.h[sl],
            start=(i == 0), stop=(i == 7)) for i, (t, sl) in enumerate(xts)]
        B.grp("tensor", fns, reads=[onescol_f] + tiles, writes=[PS[3]])
        B.op("vector", lambda h: h.tensor_scalar(
            r_m[0:1, 0:w], PS[3][0:1, 0:w], 1.0 / D, None, ALU.mult),
            reads=[PS[3]], writes=[r_m])
        for i in range(8):
            t, sl = xts[i]
            st, ssl = sq_slots[i]
            B.op("scalar", lambda h, t=t, sl=sl, st=st, ssl=ssl: h.activation(
                st.h[ssl], t.h[sl], AF.Square),
                reads=[t], writes=[st])
        fns = [lambda h, i=i: h.matmul(
            PS[3].h[0:1, 512:512 + w], onescol_f[:, :], sq_slots[i][0].h[sq_slots[i][1]],
            start=(i == 0), stop=(i == 7)) for i in range(8)]
        B.grp("tensor", fns, reads=[onescol_f] + [s for s, _ in sq_slots], writes=[PS[3]])
        B.op("vector", lambda h: h.tensor_tensor(
            r_t0[0:1, 0:w], r_m[0:1, 0:w], r_m[0:1, 0:w], ALU.mult),
            reads=[r_m], writes=[r_t0])
        B.op("vector", lambda h: h.scalar_tensor_tensor(
            r_v[0:1, 0:w], PS[3][0:1, 512:512 + w], 1.0 / D, r_t0[0:1, 0:w],
            ALU.mult, ALU.subtract),
            reads=[PS[3], r_t0], writes=[r_v])
        B.op("scalar", lambda h: h.activation(
            r_rstd[0:1, 0:w], r_v[0:1, 0:w], AF.Sqrt,
            bias=sconst[0:1, C_EPS:C_EPS + 1]),
            reads=[r_v, sconst], writes=[r_rstd])
        B.op("vector", lambda h: h.reciprocal(r_rstd[0:1, 0:w], r_rstd[0:1, 0:w]),
             reads=[r_rstd], writes=[r_rstd])
        B.op("vector", lambda h: h.tensor_tensor(
            r_mrstd[0:1, 0:w], r_m[0:1, 0:w], r_rstd[0:1, 0:w], ALU.mult),
            reads=[r_m, r_rstd], writes=[r_mrstd])
        if with_rsqv:
            B.op("scalar", lambda h: h.activation(
                r_rsqv[0:1, 0:w], r_v[0:1, 0:w], AF.Sqrt),
                reads=[r_v], writes=[r_rsqv])
            B.op("vector", lambda h: h.reciprocal(r_rsqv[0:1, 0:w], r_rsqv[0:1, 0:w]),
                 reads=[r_rsqv], writes=[r_rsqv])
            B.op("vector", lambda h: h.tensor_tensor(
                r_mrsqv[0:1, 0:w], r_m[0:1, 0:w], r_rsqv[0:1, 0:w], ALU.mult),
                reads=[r_m, r_rsqv], writes=[r_mrsqv])

    def bcast2(rowA, rowB, w, bank):
        B.grp("tensor", [lambda h: h.matmul(
            bank.h[:, 0:w], onesrow_f[0:1, 0:128], rowA.h[0:1, 0:w],
            start=True, stop=True)],
            reads=[onesrow_f, rowA], writes=[bank])
        B.grp("tensor", [lambda h: h.matmul(
            bank.h[:, 512:512 + w], onesrow_f[0:1, 0:128], rowB.h[0:1, 0:w],
            start=True, stop=True)],
            reads=[onesrow_f, rowB], writes=[bank])

    def ln_apply(xts, w, outs, bank, gamma_base=None, tmpA=None, tmpB=None):
        for i in range(8):
            xt, xsl = xts[i]
            ot, osl = outs[i]
            B.op("vector", lambda h, xt=xt, xsl=xsl: h.tensor_tensor(
                tmpA.h[:, 0:w], xt.h[xsl], bank.h[:, 0:w], ALU.mult),
                reads=[xt, bank], writes=[tmpA])
            if gamma_base is None:
                B.op("vector", lambda h, ot=ot, osl=osl: h.tensor_tensor(
                    ot.h[osl], tmpA.h[:, 0:w], bank.h[:, 512:512 + w], ALU.subtract),
                    reads=[tmpA, bank], writes=[ot])
            else:
                B.op("vector", lambda h: h.tensor_tensor(
                    tmpB.h[:, 0:w], tmpA.h[:, 0:w], bank.h[:, 512:512 + w], ALU.subtract),
                    reads=[tmpA, bank], writes=[tmpB])
                B.op("vector", lambda h, ot=ot, osl=osl, i=i: h.tensor_scalar(
                    ot.h[osl], tmpB.h[:, 0:w],
                    svecs[:, gamma_base + i:gamma_base + i + 1], None, ALU.mult),
                    reads=[tmpB, svecs], writes=[ot])

    def intfloor(zt, zsl, it, u0, d, t1, outt, osl, w):
        B.op("vector", lambda h: h.tensor_copy(it[:, 0:w], zt.h[zsl]),
             reads=[zt], writes=[it])
        B.op("vector", lambda h: h.tensor_copy(u0[:, 0:w], it[:, 0:w]),
             reads=[it], writes=[u0])
        B.op("vector", lambda h: h.tensor_tensor(d[:, 0:w], zt.h[zsl], u0[:, 0:w],
                                                 ALU.subtract),
             reads=[zt, u0], writes=[d])
        B.op("vector", lambda h: h.scalar_tensor_tensor(
            t1[:, 0:w], d[:, 0:w], 1.0, u0[:, 0:w], ALU.is_gt, ALU.add),
            reads=[d, u0], writes=[t1])
        B.op("vector", lambda h: h.tensor_scalar(
            d[:, 0:w], d[:, 0:w], 0.0, -1.0, ALU.is_lt, ALU.mult),
            reads=[d], writes=[d])
        B.op("vector", lambda h: h.tensor_tensor(outt.h[osl], t1[:, 0:w], d[:, 0:w],
                                                 ALU.add),
             reads=[t1, d], writes=[outt])

    SL512 = (slice(0, 128), slice(0, 512))

    if do_l0:
        for tt in range(4):
            s0 = tt * 512
            for i in range(8):
                B.dma("sync", "qs", A[i][:, 0:512], x_in.ap()[i, :, s0:s0 + 512],
                      writes=[A[i]])
            xts = [(A[i], SL512) for i in range(8)]
            sqs = [(Cp[i], SL512) for i in range(8)]
            stats_tile(xts, 512, sqs)
            bcast2(r_rstd, r_mrstd, 512, PS[2])
            outs = [(xnbf[i], (slice(0, 128), slice(s0, s0 + 512))) for i in range(8)]
            ln_apply(xts, 512, outs, PS[2], tmpA=oz)

        if stop == "ph1":
            for i in range(8):
                B.dma("gpsimd", "w3", x1_d.ap()[i], xnbf[i][:, :], reads=[xnbf[i]])
            B.barrier("gpsimd", "w3")
            B.emit()
            return nc
        for k in range(8):
            B.wload(ws[k], wkvT[k])

        def kv_chunk(src_tiles, sls, first, last, kb):
            fns = [lambda h, kcc=kcc, kb=kb: h.matmul(
                kb.h[:, 0:512], src_tiles[kcc].h[sls[kcc]], ws[kcc][:, 0:512],
                start=(kcc == 0), stop=(kcc == 7)) for kcc in range(8)]
            B.grp("tensor", fns, reads=list(src_tiles) + ws[:8], writes=[kb])
            fns = [lambda h, kcc=kcc, kb=kb: h.matmul(
                kb.h[:, 512:1024], src_tiles[kcc].h[sls[kcc]], ws[kcc][:, 512:1024],
                start=(kcc == 0), stop=(kcc == 7)) for kcc in range(8)]
            B.grp("tensor", fns, reads=list(src_tiles) + ws[:8], writes=[kb])
            B.op("scalar", lambda h, kb=kb: h.activation(ek[:, :], kb.h[:, 0:512], AF.Exp,
                                                  bias=cc(C_EXPS)),
                 reads=[kb, sconst], writes=[ek])
            B.op("vector", lambda h: h.tensor_reduce(
                krec[:, :], ek.h.ap().rearrange("p (hh ww) -> p hh ww", hh=8),
                axis=AX.X, op=ALU.add),
                reads=[ek], writes=[krec])
            B.op("vector", lambda h: h.reciprocal(krec[:, :], krec[:, :]),
                 reads=[krec], writes=[krec])
            for hh in range(8):
                B.op("vector", lambda h, hh=hh: h.tensor_scalar(
                    kbf[:, hh * 64:(hh + 1) * 64], ek[:, hh * 64:(hh + 1) * 64],
                    krec[:, hh:hh + 1], None, ALU.mult),
                    reads=[ek, krec], writes=[kbf])
            B.op("scalar", lambda h, kb=kb: h.activation(vbf[:, :], kb.h[:, 512:1024], AF.Copy),
                 reads=[kb], writes=[vbf])
            fns = [lambda h, hh=hh: h.matmul(
                PS[1].h[0:64, hh * 64:(hh + 1) * 64], kbf[:, hh * 64:(hh + 1) * 64],
                vbf[:, hh * 64:(hh + 1) * 64],
                start=(first and hh == 0), stop=(last and hh == 7)) for hh in range(8)]
            B.grp("tensor", fns, reads=[kbf, vbf], writes=[PS[1]])

        for tc in range(16):
            sl = (slice(0, 128), slice(tc * 128, (tc + 1) * 128))
            kv_chunk(xnbf, [sl] * 8, tc == 0, False, PS[0] if tc % 2 == 0 else PS[2])
        for og in range(4):
            for i in range(8):
                B.dma("sync", "qs", A[i][:, 0:512], xoth.ap()[i, :, og * 512:(og + 1) * 512],
                      writes=[A[i]])
            xts = [(A[i], SL512) for i in range(8)]
            sqs = [(Cp[i], SL512) for i in range(8)]
            stats_tile(xts, 512, sqs)
            bcast2(r_rstd, r_mrstd, 512, PS[2])
            outs = [(Dp[i], SL512) for i in range(8)]
            ln_apply(xts, 512, outs, PS[2], tmpA=oz)
            for tc in range(4):
                sl = (slice(0, 128), slice(tc * 128, (tc + 1) * 128))
                kv_chunk(Dp[:8], [sl] * 8, False, og == 3 and tc == 3,
                         PS[0] if tc % 2 == 0 else PS[3])
        B.op("vector", lambda h: h.tensor_copy(oq.h[0:64, 0:512], PS[1][0:64, 0:512]),
             reads=[PS[1]], writes=[oq])
        B.op("vector", lambda h: h.memset(ctx_bd[:, :], 0.0), writes=[ctx_bd])
        for pr in range(4):
            B.op("vector", lambda h, pr=pr: h.tensor_copy(
                ctx_bd.h[0:64, pr * 128:pr * 128 + 64],
                oq.h[0:64, 2 * pr * 64:2 * pr * 64 + 64]),
                reads=[oq], writes=[ctx_bd])
            B.op("vector", lambda h, pr=pr: h.tensor_copy(
                ctx_bd.h[64:128, pr * 128 + 64:pr * 128 + 128],
                oq.h[0:64, (2 * pr + 1) * 64:(2 * pr + 1) * 64 + 64]),
                reads=[oq], writes=[ctx_bd])

        if stop == "ph2":
            B.op("vector", lambda h: h.tensor_copy(A[0][:, 0:512], ctx_bd[:, :]),
                 reads=[ctx_bd], writes=[A[0]])
            B.dma("sync", "qs", x1_d.ap()[0], A[0][:, 0:1024], reads=[A[0]])
            for qq in range(4):
                B.barrier("sync", "qs" + str(qq))
            B.emit()
            return nc
        for m in range(4):
            B.wload(ws[m], wq_g.ap()[m].rearrange("k p j -> p k j"), dst_ap=ws[m].h[:, :].rearrange("p (k j) -> p k j", k=8))
        for k in range(4):
            B.wload(ws[4 + k], woutT_g[k])

        NTG = 1 if stop == "tg1" else 4
        for tg in range(NTG):
            s0 = tg * 512
            def emit_outfloor(pair, zt):
                for hf in range(2):
                    hsl = slice(hf * 512, hf * 512 + 512)
                    B.op("scalar", lambda h, zt=zt, hsl=hsl: h.activation(
                        ozi[:, 0:512], zt[:, hsl], AF.Copy),
                        reads=[zt], writes=[ozi])
                    B.op("vector", lambda h, zt=zt, hsl=hsl: h.tensor_tensor(
                        oz[:, 0:512], zt[:, hsl], ozi[:, 0:512], ALU.subtract),
                        reads=[zt, ozi], writes=[oz])
                    B.op("vector", lambda h: h.scalar_tensor_tensor(
                        od[:, 0:512], oz[:, 0:512], 1.0, ozi[:, 0:512],
                        ALU.is_gt, ALU.add),
                        reads=[oz, ozi], writes=[od])
                    B.op("vector", lambda h, pair=pair, hsl=hsl: h.scalar_tensor_tensor(
                        Dp[pair][:, hsl], oz[:, 0:512], 0.0, od[:, 0:512],
                        ALU.is_lt, ALU.subtract),
                        reads=[oz, od], writes=[Dp[pair]])
            for pair in range(16):
                zt = A[0] if pair % 2 == 0 else Cp[6]
                hbank = PS[0] if pair % 2 == 0 else PS[3]
                for half in range(2):
                    hc = 2 * pair + half
                    wt = wg[hc % 4]
                    B.wload(wt, w1_g.ap()[hc].rearrange("k p j -> p k j"), dst_ap=wt.h[:, :].rearrange("p (k j) -> p k j", k=8))
                    bsl = slice(half * 512, half * 512 + 512)
                    fns = [lambda h, kcc=kcc, wt=wt, bsl=bsl, s0=s0, hbank=hbank: h.matmul(
                        hbank.h[:, bsl], wt[:, kcc * 128:(kcc + 1) * 128],
                        xnbf[kcc][:, s0:s0 + 512], start=(kcc == 0), stop=(kcc == 7))
                        for kcc in range(8)]
                    B.grp("tensor", fns, reads=[wt] + xnbf, writes=[hbank])
                    B.op("scalar", lambda h, bsl=bsl, hc=hc, hbank=hbank, zt=zt: h.activation(
                        zt[:, bsl], hbank.h[:, bsl], AF.Identity,
                        bias=b1cc[:, hc:hc + 1], scale=cc(C_256)),
                        reads=[hbank, b1cc, sconst], writes=[zt])
                B.op("scalar", lambda h, zt=zt: h.activation(
                    zti[:, :], zt[:, :], AF.Copy),
                    reads=[zt], writes=[zti])
                B.op("vector", lambda h, zt=zt: h.tensor_tensor(
                    A[7][:, :], zt[:, :], zti[:, :], ALU.subtract),
                    reads=[zt, zti], writes=[A[7]])
                for hf, mt in ((0, od), (1, ot1)):
                    hsl = slice(hf * 512, hf * 512 + 512)
                    B.op("vector", lambda h, mt=mt, hsl=hsl: h.scalar_tensor_tensor(
                        mt[:, 0:512], A[7][:, hsl], 1.0, zti[:, hsl],
                        ALU.is_gt, ALU.add),
                        reads=[A[7], zti], writes=[mt])
                for hf, mt in ((0, od), (1, ot1)):
                    hsl = slice(hf * 512, hf * 512 + 512)
                    B.op("vector", lambda h, mt=mt, hsl=hsl: h.scalar_tensor_tensor(
                        A[4][:, hsl], A[7][:, hsl], 0.0, mt[:, 0:512],
                        ALU.is_lt, ALU.subtract),
                        reads=[A[7], mt], writes=[A[4]])
                if pair >= 1:
                    emit_outfloor(pair - 1, Cp[6] if pair % 2 == 0 else A[0])
                B.op("scalar", lambda h: h.activation(
                    A[7][:, :], A[4][:, :], AF.Abs),
                    reads=[A[4]], writes=[A[7]])
                B.op("vector", lambda h: h.tensor_scalar(
                    A[6][:, :], A[4][:, :], 740.0, -PWL_DFIX, ALU.is_equal,
                    ALU.mult),
                    reads=[A[4]], writes=[A[6]])
                rrot = (A[1], A[2], A[3], Cp[5])
                dve_pos = False
                pool_sign = 0.0
                for ti, ((srt, _c, _U), (cci, cbi, sgn)) in enumerate(
                        zip(PWL_GE, pwl_cols)):
                    r = rrot[ti % 4]
                    src = A[7] if srt == "a" else A[4]
                    B.op("scalar", lambda h, cci=cci, cbi=cbi, r=r, src=src:
                        h.activation(r[:, :], src[:, :], AF.Relu,
                                     bias=cc(cbi), scale=cc(cci)),
                        reads=[src, sconst], writes=[r])
                    if ti in PWL_POOL:
                        if pool_sign == 0.0:
                            pool_sign = sgn
                            B.op("gpsimd", lambda h, r=r: h.tensor_copy(
                                pacc[:, :], r[:, :]),
                                reads=[r], writes=[pacc])
                        else:
                            op = ALU.add if sgn == pool_sign else ALU.subtract
                            B.op("gpsimd", lambda h, r=r, op=op: h.tensor_tensor(
                                pacc[:, :], pacc[:, :], r[:, :], op),
                                reads=[r], writes=[pacc])
                    elif sgn > 0:
                        if not dve_pos:
                            dve_pos = True
                            B.op("vector", lambda h, r=r: h.tensor_scalar(
                                A[5][:, :], r[:, :], 0.5, None, ALU.add),
                                reads=[r], writes=[A[5]])
                        else:
                            B.op("vector", lambda h, r=r: h.tensor_tensor(
                                A[5][:, :], A[5][:, :], r[:, :], ALU.add),
                                reads=[r], writes=[A[5]])
                    else:
                        B.op("vector", lambda h, r=r: h.tensor_tensor(
                            A[6][:, :], A[6][:, :], r[:, :], ALU.add),
                            reads=[r], writes=[A[6]])
                B.op("vector", lambda h: h.tensor_tensor(
                    A[3][:, :], A[5][:, :], A[6][:, :], ALU.subtract),
                    reads=[A[5], A[6]], writes=[A[3]])
                if pool_sign != 0.0:
                    pop = ALU.add if pool_sign > 0 else ALU.subtract
                    B.op("vector", lambda h, pop=pop: h.tensor_tensor(
                        A[3][:, :], A[3][:, :], pacc[:, :], pop),
                        reads=[pacc], writes=[A[3]])
                B.op("vector", lambda h, zt=zt: h.scalar_tensor_tensor(
                    zt[:, :], A[4][:, :], -0.5, A[3][:, :], ALU.mult, ALU.add),
                    reads=[A[4], A[3]], writes=[zt])
            emit_outfloor(15, Cp[6])
            w2bank = [PS[1], PS[1], PS[2], PS[2], PS[0], PS[0], PS[3], PS[3]]
            for hc in range(32):
                wt = wg[hc % 4]
                B.wload(wt, w2_g[hc])
                pairt = Dp[hc // 2]
                hsl = slice((hc % 2) * 512, (hc % 2) * 512 + 512)
                fns = []
                for dmc in range(8):
                    bank = w2bank[dmc]
                    bsl = slice((dmc % 2) * 512, (dmc % 2) * 512 + 512)
                    fns.append(lambda h, wt=wt, dmc=dmc, bank=bank, bsl=bsl,
                               pairt=pairt, hsl=hsl, hc=hc: h.matmul(
                        bank.h[:, bsl], wt[:, dmc * 128:(dmc + 1) * 128],
                        pairt.h[:, hsl], start=(hc == 0), stop=(hc == 31)))
                B.grp("tensor", fns, reads=[wt, pairt],
                      writes=[PS[0], PS[1], PS[2], PS[3]])
            for dmc in (4, 5, 6, 7, 0, 1, 2, 3):
                bank = w2bank[dmc]
                bsl = slice((dmc % 2) * 512, (dmc % 2) * 512 + 512)
                B.op("vector", lambda h, bank=bank, bsl=bsl, dmc=dmc: h.tensor_scalar(
                    oz[:, :], bank.h[:, bsl], -1.0, b2cc[:, dmc:dmc + 1],
                    ALU.mult, ALU.add),
                    reads=[bank, b2cc], writes=[oz])
                intfloor(oz, SL512, ozi, ou0, od, ot1, oq, SL512, 512)
                B.op("scalar", lambda h, dmc=dmc: h.activation(
                    Cp[dmc][:, 0:512], oq[:, 0:512], AF.Silu, scale=cc(C_INV256)),
                    reads=[oq, sconst], writes=[Cp[dmc]])
            for mc in range(4):
                bank = PS[0 if mc < 2 else 3]
                bsl = slice((mc % 2) * 512, (mc % 2) * 512 + 512)
                fns = [lambda h, kcc=kcc, mc=mc, bank=bank, bsl=bsl, s0=s0: h.matmul(
                    bank.h[:, bsl], ws[mc][:, kcc * 128:(kcc + 1) * 128],
                    xnbf[kcc][:, s0:s0 + 512], start=(kcc == 0), stop=(kcc == 7))
                    for kcc in range(8)]
                B.grp("tensor", fns, reads=[ws[mc]] + xnbf, writes=[bank])
                B.op("scalar", lambda h, mc=mc, bank=bank, bsl=bsl: h.activation(
                    qbf[mc][:, :], bank.h[:, bsl], AF.Copy),
                    reads=[bank], writes=[qbf[mc]])
            for pr in range(4):
                bank = PS[0 if pr < 2 else 3]
                bsl = slice((pr % 2) * 512, (pr % 2) * 512 + 512)
                B.grp("tensor", [lambda h, pr=pr, bank=bank, bsl=bsl: h.matmul(
                    bank.h[:, bsl], ctx_bd[:, pr * 128:(pr + 1) * 128], qbf[pr][:, :],
                    start=True, stop=True)],
                    reads=[ctx_bd, qbf[pr]], writes=[bank])
                B.op("scalar", lambda h, pr=pr, bank=bank, bsl=bsl: h.activation(
                    aobf[pr][:, :], bank.h[:, bsl], AF.Copy),
                    reads=[bank], writes=[aobf[pr]])
            for dmc in range(8):
                bank = PS[1 + dmc % 2]
                bsl = slice(0, 512) if dmc < 4 else slice(512, 1024)
                fns = [lambda h, kcc=kcc, dmc=dmc, bank=bank, bsl=bsl: h.matmul(
                    bank.h[:, bsl], ws[4 + kcc][:, dmc * 128:(dmc + 1) * 128],
                    aobf[kcc][:, :], start=(kcc == 0), stop=(kcc == 3))
                    for kcc in range(4)]
                B.grp("tensor", fns, reads=ws[4:8] + aobf, writes=[bank])
                B.dma("sync", "qs", A[7][:, 0:512], x_in.ap()[dmc, :, s0:s0 + 512],
                      writes=[A[7]])
                B.op("vector", lambda h, bank=bank, bsl=bsl, dmc=dmc: h.tensor_scalar(
                    oz[:, :], bank.h[:, bsl],
                    svecs[:, VB_OUT + dmc:VB_OUT + dmc + 1], None, ALU.add),
                    reads=[bank, svecs], writes=[oz])
                B.op("vector", lambda h, dmc=dmc: h.tensor_tensor(
                    ot1[:, :], oz[:, 0:512], Cp[dmc][:, 0:512], ALU.mult),
                    reads=[oz, Cp[dmc]], writes=[ot1])
                B.op("vector", lambda h: h.tensor_tensor(
                    og1[:, :], A[7][:, 0:512], ot1[:, 0:512], ALU.add),
                    reads=[A[7], ot1], writes=[og1])
                B.dma("sync", "qs", x1_d.ap()[dmc, :, s0:s0 + 512], og1[:, 0:512],
                      reads=[og1], writes=[dt("x1")])

    if stop == "tg1":
        for qq in range(4):
            B.barrier("sync", "qs" + str(qq))
        B.emit()
        return nc
    if do_l1:
        for tt in range(4):
            s0 = tt * 512
            for i in range(8):
                B.dma("sync", "qs", A[i][:, 0:512], x1_d.ap()[i, :, s0:s0 + 512],
                      reads=[dt("x1")], writes=[A[i]])
            xts = [(A[i], SL512) for i in range(8)]
            sqs = [(Cp[i], SL512) for i in range(8)]
            stats_tile(xts, 512, sqs, with_rsqv=True)
            bcast2(r_rstd, r_mrstd, 512, PS[2])
            outs = [(xnbf[i], (slice(0, 128), slice(s0, s0 + 512))) for i in range(8)]
            ln_apply(xts, 512, outs, PS[2], tmpA=oz)
            bcast2(r_rsqv, r_mrsqv, 512, PS[2])
            outs2 = [(Dp[2 * i + tt // 2],
                      (slice(0, 128), slice((tt % 2) * 512, (tt % 2) * 512 + 512)))
                     for i in range(8)]
            ln_apply(xts, 512, outs2, PS[2], gamma_base=VGAM1, tmpA=oz, tmpB=ou0)

        def xnn_sl(kc, a, b):
            half = a // 1024
            return Dp[2 * kc + half], (slice(0, 128), slice(a - half * 1024, b - half * 1024))

        def qkva_mm(wt, bank, bsl, a, b):
            fns = []
            rds = [wt]
            for kcc in range(8):
                t, sl = xnn_sl(kcc, a, b)
                rds.append(t)
                fns.append(lambda h, t=t, sl=sl, kcc=kcc: h.matmul(
                    bank.h[:, bsl], wt[:, kcc * 128:(kcc + 1) * 128], t.h[sl],
                    start=(kcc == 0), stop=(kcc == 7)))
            B.grp("tensor", fns, reads=rds, writes=[bank])

        def scan_phase(ccn, full):
            wt = wg[0]
            B.wload(wt, qkva_g.ap()[8 + ccn].rearrange("k p j -> p k j"), dst_ap=wt.h[:, :].rearrange("p (k j) -> p k j", k=8))
            wt2 = wg[1]
            B.wload(wt2, qkva_g.ap()[16 + ccn].rearrange("k p j -> p k j"), dst_ap=wt2.h[:, :].rearrange("p (k j) -> p k j", k=8))
            for tt in range(4):
                bsl = slice((tt % 2) * 512, (tt % 2) * 512 + 512)
                asl = slice((tt % 2) * 512, (tt % 2) * 512 + 512)
                qkva_mm(wt, PS[0], bsl, tt * 512, (tt + 1) * 512)
                B.op("scalar", lambda h, bsl=bsl, tt=tt, asl=asl: h.activation(
                    A[tt // 2].h[:, asl], PS[0].h[:, bsl], AF.Copy),
                    reads=[PS[0]], writes=[A[tt // 2]])
                qkva_mm(wt2, PS[1], bsl, tt * 512, (tt + 1) * 512)
                B.op("scalar", lambda h, bsl=bsl, tt=tt, asl=asl: h.activation(
                    A[2 + tt // 2].h[:, asl], PS[1].h[:, bsl], AF.Sigmoid),
                    reads=[PS[1]], writes=[A[2 + tt // 2]])
            B.op("vector", lambda h: h.tensor_tensor_scan(
                A[4][:, :], A[2][:, :], A[0][:, :], 0.0, ALU.mult, ALU.add),
                reads=[A[2], A[0]], writes=[A[4]])
            B.op("vector", lambda h: h.tensor_tensor_scan(
                A[5][:, :], A[3][:, :], A[1][:, :], A[4][:, 1023:1024],
                ALU.mult, ALU.add),
                reads=[A[3], A[1], A[4]], writes=[A[5]])
            if full:
                B.op("vector", lambda h: h.tensor_tensor_scan(
                    A[6][:, :], A[2][:, :], A[2][:, :], 1.0, ALU.mult, ALU.bypass),
                    reads=[A[2]], writes=[A[6]])
                B.op("vector", lambda h: h.tensor_tensor_scan(
                    A[7][:, :], A[3][:, :], A[3][:, :], A[6][:, 1023:1024],
                    ALU.mult, ALU.bypass),
                    reads=[A[3], A[6]], writes=[A[7]])

        for ccn in range(8):
            scan_phase(ccn, full=False)
            B.op("vector", lambda h, ccn=ccn: h.tensor_copy(
                carry_sb[:, ccn:ccn + 1], A[5][:, 1023:1024]),
                reads=[A[5]], writes=[carry_sb])
        for ccn in range(8):
            B.dma("gpsimd", "w3",
                  carry_in.ap().rearrange("o (c p) -> (o c) p", p=128)[ccn]
                  .rearrange("p -> p ()"),
                  carry_sb[:, ccn:ccn + 1], reads=[carry_sb], writes=[dt("cc_in")])
        B.op("gpsimd", lambda h: h.collective_compute(
            "AllGather", ALU.bypass,
            replica_groups=[[0, 1], [2, 3], [4, 5], [6, 7]],
            ins=[carry_in.ap().opt()], outs=[carry_out.ap().opt()]),
            reads=[dt("cc_in")], writes=[dt("cc_out")], chkey="cc")
        B.barrier("gpsimd", "cc")
        for ccn in range(8):
            B.dma("gpsimd", "w3", carryin_sb[:, ccn:ccn + 1],
                  carry_out.ap()[0:1].rearrange("o (c p) -> (o c) p", p=128)[ccn]
                  .rearrange("p -> p ()"),
                  reads=[dt("cc_out")], writes=[carryin_sb])
        B.op("vector", lambda h: h.tensor_scalar(
            carryin_sb[:, :], carryin_sb[:, :], role_sb[:, 0:1], None, ALU.mult),
            reads=[carryin_sb, role_sb], writes=[carryin_sb])

        for ccn in range(8):
            scan_phase(ccn, full=True)
            B.op("vector", lambda h, ccn=ccn: h.scalar_tensor_tensor(
                A[4][:, :], A[6][:, :], carryin_sb[:, ccn:ccn + 1], A[4][:, :],
                ALU.mult, ALU.add),
                reads=[A[6], carryin_sb], writes=[A[4]])
            B.op("vector", lambda h, ccn=ccn: h.scalar_tensor_tensor(
                A[5][:, :], A[7][:, :], carryin_sb[:, ccn:ccn + 1], A[5][:, :],
                ALU.mult, ALU.add),
                reads=[A[7], carryin_sb], writes=[A[5]])
            wt3 = wg[2]
            B.wload(wt3, qkva_g.ap()[ccn].rearrange("k p j -> p k j"), dst_ap=wt3.h[:, :].rearrange("p (k j) -> p k j", k=8))
            for tt in range(4):
                bsl = slice((tt % 2) * 512, (tt % 2) * 512 + 512)
                qkva_mm(wt3, PS[2], bsl, tt * 512, (tt + 1) * 512)
                yt = A[4 + tt // 2]
                ysl = slice((tt % 2) * 512, (tt % 2) * 512 + 512)
                B.op("vector", lambda h, bsl=bsl, yt=yt, ysl=ysl: h.tensor_tensor(
                    yt.h[:, ysl], PS[2].h[:, bsl], yt.h[:, ysl], ALU.mult),
                    reads=[PS[2], yt], writes=[yt])
            B.dma("sync", "qs", ssp.ap()[ccn, :, 0:1024], A[4][:, :],
                  reads=[A[4]], writes=[dt("ssp")])
            B.dma("sync", "qs", ssp.ap()[ccn, :, 1024:2048], A[5][:, :],
                  reads=[A[5]], writes=[dt("ssp")])

        for k in range(8):
            B.wload(ws[k], shared_g[k])
        for tg in range(2):
            t0 = tg * 1024
            for tq in range(8):
                tc = tg * 8 + tq
                gbank = PS[tq % 2]
                fns = [lambda h, kcc=kcc, tc=tc, gbank=gbank: h.matmul(
                    gbank.h[:, 0:8], xnbf[kcc][:, tc * 128:(tc + 1) * 128],
                    gx[:, kcc * 8:(kcc + 1) * 8], start=(kcc == 0), stop=(kcc == 7))
                    for kcc in range(8)]
                B.grp("tensor", fns, reads=xnbf + [gx], writes=[gbank])
                B.op("scalar", lambda h, gbank=gbank: h.activation(
                    p8[:, :], gbank.h[:, 0:8], AF.Exp, bias=cc(C_G8)),
                    reads=[gbank, sconst], writes=[p8])
                B.op("vector", lambda h: h.tensor_reduce(
                    m1c[:, :], p8[:, :], axis=AX.X, op=ALU.add),
                    reads=[p8], writes=[m1c])
                B.op("vector", lambda h: h.reciprocal(m1c[:, :], m1c[:, :]),
                     reads=[m1c], writes=[m1c])
                B.op("vector", lambda h: h.tensor_scalar(
                    p8[:, :], p8[:, :], m1c[:, 0:1], None, ALU.mult),
                    reads=[p8, m1c], writes=[p8])
                B.op("vector", lambda h: h.tensor_reduce(
                    m1c[:, :], p8[:, :], axis=AX.X, op=ALU.max),
                    reads=[p8], writes=[m1c])
                B.op("vector", lambda h: h.tensor_scalar(
                    p8b[:, :], p8[:, :], m1c[:, 0:1], -2.0, ALU.is_ge, ALU.mult),
                    reads=[p8, m1c], writes=[p8b])
                B.op("vector", lambda h: h.tensor_tensor(
                    p8b[:, :], p8[:, :], p8b[:, :], ALU.add),
                    reads=[p8, p8b], writes=[p8b])
                B.op("vector", lambda h: h.tensor_reduce(
                    m2c[:, :], p8b[:, :], axis=AX.X, op=ALU.max),
                    reads=[p8b], writes=[m2c])
                B.op("vector", lambda h: h.tensor_scalar(
                    p8b[:, :], p8[:, :], m2c[:, 0:1], None, ALU.is_ge),
                    reads=[p8, m2c], writes=[p8b])
                B.op("vector", lambda h: h.tensor_tensor(
                    p8b[:, :], p8b[:, :], p8[:, :], ALU.mult),
                    reads=[p8b, p8], writes=[p8b])
                B.op("vector", lambda h: h.tensor_reduce(
                    m2c[:, :], p8b[:, :], axis=AX.X, op=ALU.add),
                    reads=[p8b], writes=[m2c])
                B.op("vector", lambda h: h.reciprocal(m2c[:, :], m2c[:, :]),
                     reads=[m2c], writes=[m2c])
                B.op("vector", lambda h, tq=tq: h.tensor_scalar(
                    comb[tq][:, :], p8b[:, :], m2c[:, 0:1], None, ALU.mult),
                    reads=[p8b, m2c], writes=[comb[tq]])
            for tq in range(8):
                B.grp("tensor", [lambda h, tq=tq: h.transpose(
                    PS[3].h[0:8, 0:128], comb[tq][:, :], sident_f[:, :])],
                    reads=[comb[tq], sident_f], writes=[PS[3]])
                B.op("vector", lambda h, tq=tq: h.tensor_copy(
                    cbstage[:, tq * 128:(tq + 1) * 128], PS[3][0:8, 0:128]),
                    reads=[PS[3]], writes=[cbstage])
            B.dma("sync", "qs", cb_d[:, :], cbstage[:, :],
                  reads=[cbstage], writes=[dt("cb_d")])

            for hf in range(2):
                th0 = t0 + hf * 512
                for g in range(2):
                    for dq in range(4):
                        dmc = g * 4 + dq
                        bank = PS[dq]
                        bsl = slice(hf * 512, hf * 512 + 512)
                        fns = [lambda h, kcc=kcc, dmc=dmc, bank=bank, bsl=bsl,
                               th0=th0: h.matmul(
                            bank.h[:, bsl], ws[kcc][:, dmc * 128:(dmc + 1) * 128],
                            xnbf[kcc][:, th0:th0 + 512],
                            start=(kcc == 0), stop=(kcc == 7)) for kcc in range(8)]
                        B.grp("tensor", fns, reads=ws[:8] + xnbf, writes=[bank])
                        B.op("scalar", lambda h, bank=bank, bsl=bsl, dmc=dmc: h.activation(
                            gf[:, 0:512], bank.h[:, bsl], AF.Gelu,
                            bias=svecs[:, VSHB + dmc:VSHB + dmc + 1]),
                            reads=[bank, svecs], writes=[gf])
                        B.op("vector", lambda h, dmc=dmc, hf=hf: h.tensor_scalar(
                            Cp[dmc][:, hf * 512:hf * 512 + 512], gf[:, 0:512], 0.5,
                            None, ALU.mult),
                            reads=[gf], writes=[Cp[dmc]])

            for e in range(NE):
                B.dma("gpsimd", "w3", cbrow_bf[0:1, :], cb_d[e:e + 1, :],
                      reads=[dt("cb_d")], writes=[cbrow_bf])
                for hf2 in range(2):
                    B.grp("tensor", [lambda h, hf2=hf2: h.matmul(
                        PS[3].h[:, hf2 * 512:hf2 * 512 + 512], onesrow_bf[0:1, 0:128],
                        cbrow_bf[0:1, hf2 * 512:hf2 * 512 + 512],
                        start=True, stop=True)],
                        reads=[onesrow_bf, cbrow_bf], writes=[PS[3]])
                B.op("vector", lambda h: h.tensor_copy(cbb[:, :], PS[3][:, :]),
                     reads=[PS[3]], writes=[cbb])
                for half in range(2):
                    for hcq in range(16):
                        hc = half * 16 + hcq
                        wt = wg[hcq % 4]
                        abank = PS[hcq % 2]
                        B.wload(wt, w1e_g.ap()[e, hc].rearrange("k p j -> p k j"), dst_ap=wt.h[:, :].rearrange("p (k j) -> p k j", k=8))
                        for hf in range(2):
                            bsl = slice(hf * 512, hf * 512 + 512)
                            fns = [lambda h, kcc=kcc, wt=wt, bsl=bsl, hf=hf, t0=t0,
                                   abank=abank: h.matmul(
                                abank.h[:, bsl], wt[:, kcc * 128:(kcc + 1) * 128],
                                xnbf[kcc][:, t0 + hf * 512:t0 + hf * 512 + 512],
                                start=(kcc == 0), stop=(kcc == 7)) for kcc in range(8)]
                            B.grp("tensor", fns, reads=[wt] + xnbf, writes=[abank])
                        B.op("scalar", lambda h, hcq=hcq, hc=hc, e=e, abank=abank: h.activation(
                            Dp[hcq][:, :], abank[:, :], AF.Gelu,
                            bias=b1e_s[:, e * 32 + hc:e * 32 + hc + 1]),
                            reads=[abank, b1e_s], writes=[Dp[hcq]])
                    for g in range(2):
                        for hcq in range(16):
                            hc = half * 16 + hcq
                            wt = wg[(hcq + g * 2) % 4]
                            B.wload(wt, w2e_g.ap()[e, hc])
                            fns = []
                            for dq in range(4):
                                dmc = g * 4 + dq
                                for hf in range(2):
                                    fns.append(lambda h, wt=wt, dmc=dmc, dq=dq, hf=hf,
                                               hcq=hcq: h.matmul(
                                        PS[dq].h[:, hf * 512:hf * 512 + 512],
                                        wt[:, dmc * 128:(dmc + 1) * 128],
                                        Dp[hcq][:, hf * 512:hf * 512 + 512],
                                        start=(hcq == 0), stop=(hcq == 15)))
                            B.grp("tensor", fns,
                                  reads=[wt] + [Dp[q] for q in range(16)],
                                  writes=[PS[0], PS[1], PS[2], PS[3]])
                        for dq in range(4):
                            dmc = g * 4 + dq
                            B.op("vector", lambda h, dq=dq, dmc=dmc, e=e: h.scalar_tensor_tensor(
                                gf2[:, :], PS[dq].h[:, :],
                                b2c_s[:, e * 8 + dmc:e * 8 + dmc + 1],
                                cbb[:, :], ALU.add, ALU.mult),
                                reads=[PS[dq], b2c_s, cbb], writes=[gf2])
                            B.op("vector", lambda h, dmc=dmc: h.tensor_tensor(
                                Cp[dmc][:, :], Cp[dmc][:, :], gf2[:, :], ALU.add),
                                reads=[gf2], writes=[Cp[dmc]])

            for sub in range(2):
                u0 = sub * 512
                xts = [(Cp[i], (slice(0, 128), slice(u0, u0 + 512))) for i in range(8)]
                sqs = [(A[i // 2], (slice(0, 128), slice((i % 2) * 512, (i % 2) * 512 + 512)))
                       for i in range(8)]
                stats_tile(xts, 512, sqs)
                bcast2(r_rstd, r_mrstd, 512, PS[0])
                for ci in range(8):
                    B.op("vector", lambda h, ci=ci, u0=u0: h.tensor_tensor(
                        gf[:, 0:512], Cp[ci][:, u0:u0 + 512], PS[0][:, 0:512], ALU.mult),
                        reads=[Cp[ci], PS[0]], writes=[gf])
                    B.op("vector", lambda h: h.tensor_tensor(
                        gf[:, 0:512], gf[:, 0:512], PS[0][:, 512:1024], ALU.subtract),
                        reads=[gf, PS[0]], writes=[gf])
                    B.op("vector", lambda h, ci=ci: h.tensor_scalar(
                        gf[:, 0:512], gf[:, 0:512],
                        svecs[:, VGAMF + ci:VGAMF + ci + 1],
                        svecs[:, VBETF + ci:VBETF + ci + 1], ALU.mult, ALU.add),
                        reads=[gf, svecs], writes=[gf])
                    B.op("scalar", lambda h: h.activation(
                        gf2[:, 0:512], gf[:, 0:512], AF.Silu),
                        reads=[gf], writes=[gf2])
                    B.dma("sync", "qs", A[6][:, 0:512],
                          x1_d.ap()[ci, :, t0 + u0:t0 + u0 + 512],
                          reads=[dt("x1")], writes=[A[6]])
                    B.dma("sync", "qs", A[7][:, 0:512],
                          ssp.ap()[ci, :, t0 + u0:t0 + u0 + 512],
                          reads=[dt("ssp")], writes=[A[7]])
                    B.op("vector", lambda h: h.tensor_tensor(
                        gf2[:, 0:512], A[7][:, 0:512], gf2[:, 0:512], ALU.mult),
                        reads=[A[7], gf2], writes=[gf2])
                    B.op("vector", lambda h: h.tensor_tensor(
                        gf2[:, 0:512], A[6][:, 0:512], gf2[:, 0:512], ALU.add),
                        reads=[A[6], gf2], writes=[gf2])
                    B.dma("sync", "qs", out_d.ap()[ci, :, t0 + u0:t0 + u0 + 512],
                          gf2[:, 0:512], reads=[gf2])
    if do_l1 or part == "l0":
        for qq in range(4):
            B.barrier("sync", "qs" + str(qq))
    B.emit()
    return nc



def _shared_inputs(inp):
    f = np.float32
    asc = np.ascontiguousarray
    d = {}
    w_qkv = inp["L0_attn_w_qkv"]
    wqT = asc(w_qkv[:512].T)
    d["wq_g"] = asc(wqT.reshape(8, 128, 4, 128).transpose(2, 0, 1, 3))
    d["wkvT"] = asc(w_qkv[512:1536].T.reshape(8, 128, 1024))
    d["woutT_g"] = asc(inp["L0_attn_w_out"].T.reshape(4, 128, 1024))
    w1 = inp["L0_ffn_w1"]
    d["w1_g"] = asc(w1.reshape(8, 128, 32, 128).transpose(2, 0, 1, 3))
    d["w2_g"] = asc(inp["L0_ffn_w2"].reshape(32, 128, 1024))
    d["b1l0"] = asc(inp["L0_ffn_b1"].reshape(32, 128).T)
    d["qkva_g"] = asc(inp["L1_attn_W_qkva"].T.reshape(8, 128, 24, 128)
                      .transpose(2, 0, 1, 3))
    d["gate_g"] = asc(inp["L1_ffn_gate_W"].reshape(8, 128, 8))
    d["shared_g"] = asc(inp["L1_ffn_shared_W"].reshape(8, 128, 1024))
    d["w1e_g"] = asc(inp["L1_ffn_W1"].reshape(8, 8, 128, 32, 128)
                     .transpose(0, 3, 1, 2, 4))
    d["w2e_g"] = asc(inp["L1_ffn_W2"].reshape(8, 32, 128, 1024))
    d["b1e"] = asc(inp["L1_ffn_b1"].reshape(8, 32, 128).transpose(2, 0, 1)
                   .reshape(128, 256))
    d["b2c"] = asc(inp["L1_ffn_b2"].reshape(8, 8, 128).transpose(2, 0, 1)
                   .reshape(128, 64))
    vecs = np.zeros((128, 64), f)
    vecs[:, VB_OUT:VB_OUT + 8] = inp["L0_attn_b_out"].reshape(8, 128).T
    vecs[:, VB2L0:VB2L0 + 8] = inp["L0_ffn_b2"].reshape(8, 128).T
    vecs[:, VGAM1:VGAM1 + 8] = inp["L1_attn_gamma"].reshape(8, 128).T
    vecs[:, VSHB:VSHB + 8] = inp["L1_ffn_shared_b"].reshape(8, 128).T
    vecs[:, VGAMF:VGAMF + 8] = inp["L1_ffn_gamma"].reshape(8, 128).T
    vecs[:, VBETF:VBETF + 8] = inp["L1_ffn_beta"].reshape(8, 128).T
    d["vecs"] = vecs
    cvals = _const_layout()[0]
    NCON = max(64, len(cvals))
    consts = np.zeros((128, NCON), f)
    for i, v in enumerate(cvals):
        consts[:, i] = v
    d["consts"] = consts
    d["ident"] = np.eye(128, dtype=f)
    return {k: np.asarray(v, f) for k, v in d.items()}


_L0_KEYS = ("xown", "xoth", "wq_g", "wkvT", "woutT_g", "w1_g", "w2_g", "b1l0",
            "vecs", "consts", "ident")
_L1_KEYS = ("qkva_g", "gate_g", "shared_g", "w1e_g", "w2e_g", "b1e", "b2c",
            "role", "vecs", "consts", "ident", "x1")


def kernel(**inp):
    inp = {k: np.asarray(v) for k, v in inp.items()}
    shared = _shared_inputs(inp)
    x = np.asarray(inp["L0_attn_x"], np.float32)
    cores = []
    for c in range(8):
        b, hh = c // 2, c % 2
        xo = np.ascontiguousarray(x[b, hh * 2048:(hh + 1) * 2048, :].T
                                  .reshape(8, 128, T))
        xt = np.ascontiguousarray(x[b, (1 - hh) * 2048:(2 - hh) * 2048, :].T
                                  .reshape(8, 128, T))
        role = np.full((128, 1), float(hh), np.float32)
        cores.append(dict(xown=xo, xoth=xt, role=role))

    if PART == "full":
        nc = build_program("full")
        in_maps = []
        for c in range(8):
            m = dict(shared)
            m.update({k: v for k, v in cores[c].items()})
            in_maps.append(m)
        res = run_bass_kernel_spmd(nc, in_maps, core_ids=list(range(8)))
        outs = [r["out"] for r in res.results]
    else:
        nc0 = build_program("l0")
        in_maps = []
        for c in range(8):
            m = {k: shared[k] for k in _L0_KEYS if k in shared}
            m["xown"] = cores[c]["xown"]
            m["xoth"] = cores[c]["xoth"]
            in_maps.append(m)
        res0 = run_bass_kernel_spmd(nc0, in_maps, core_ids=list(range(8)))
        x1s = [np.asarray(r["x1"], np.float32) for r in res0.results]
        nc1 = build_program("l1")
        in_maps = []
        for c in range(8):
            m = {k: shared[k] for k in _L1_KEYS if k in shared}
            m["role"] = cores[c]["role"]
            m["x1"] = x1s[c]
            in_maps.append(m)
        res = run_bass_kernel_spmd(nc1, in_maps, core_ids=list(range(8)))
        outs = [r["out"] for r in res.results]

    full = np.empty((4, 4096, 1024), np.float32)
    for c in range(8):
        b, hh = c // 2, c % 2
        o = np.asarray(outs[c], np.float32).reshape(1024, T)
        full[b, hh * 2048:(hh + 1) * 2048, :] = o.T
    return full


if __name__ == "__main__":
    import sys
    part = sys.argv[1] if len(sys.argv) > 1 else "l0"
    nc = build_program(part)
    print("built", part)


# version 18 (final)
# speedup vs baseline: 1.2367x; 1.0065x over previous
import numpy as np
import ml_dtypes
import concourse.bass as bass
import concourse.mybir as mybir
from concourse.bass_utils import run_bass_kernel_spmd

F32 = mybir.dt.float32
BF16 = mybir.dt.bfloat16
I32 = mybir.dt.int32
AF = mybir.ActivationFunctionType
ALU = mybir.AluOpType
AX = mybir.AxisListType

T = 2048
D = 1024
CH = 8
NE = 8
PART = "full"

_XBITS = [3229614080, 3228565504, 3227516928, 3226468352, 3225419776, 3224371200,
          3223322624, 3222274048, 3221225472, 3219128320, 3217031168, 3214934016,
          3212836864, 3208642560, 3204448256, 3196059648, 0, 1048576000, 1056964608,
          1061158912, 1065353216, 1067450368, 1069547520, 1071644672, 1073741824,
          1074790400, 1075838976, 1076887552, 1077936128, 1078984704, 1080033280,
          1081081856, 1082130432]
_YBITS = [3104096256, 3115170560, 3126160128, 3136671360, 3146036032, 3154526592,
          3162396960, 3168883268, 3174719104, 3180302958, 3184343880, 3188144980,
          3189929606, 3190688958, 3189635234, 3184359005, 0, 1041843410, 1051788207,
          1058307280, 1062691422, 1066342549, 1068706892, 1071056601, 1073360140,
          1074675036, 1075773863, 1076853182, 1077919142, 1078976838, 1080029866,
          1081080467, 1082129901]


def _pwl_terms():
    X = np.array(_XBITS, np.uint32).view(np.float32).astype(np.float64)
    Y = np.array(_YBITS, np.uint32).view(np.float32).astype(np.float64)
    s = (Y[1:] - Y[:-1]) / 0.25
    c = np.empty(33)
    c[0] = s[0]
    c[1:32] = s[1:] - s[:-1]
    c[32] = 1.0 - s[31]
    U = np.concatenate([X[:32] * 256.0, [1024.0]])
    terms = [(float(c[i]), float(U[i]), +1.0) for i in range(33)]
    d = 256.0 * Y[0]
    terms += [(-d, 1024.0, +1.0), (d, 1025.0, +1.0),
              (-d, 1024.0, -1.0), (d, 1025.0, -1.0)]
    const = 256.0 * Y[0] + 0.5
    return terms, const

PWL_TERMS, PWL_CONST = _pwl_terms()

def _pwl_terms_ge():
    X = np.array(_XBITS, np.uint32).view(np.float32).astype(np.float64)
    Y = np.array(_YBITS, np.uint32).view(np.float32).astype(np.float64)
    t = X[16:]
    G = Y[16:] - t / 2.0
    sG = (G[1:] - G[:-1]) / 0.25
    cG = np.empty(17)
    cG[0] = sG[0]
    cG[1:16] = sG[1:] - sG[:-1]
    cG[16] = 0.5 - sG[15]
    UG = np.concatenate([t[:16] * 256.0, [1024.0]])
    dstep = 256.0 * (4.0 - Y[32])
    Ev = np.array([Y[j] - (X[j] / 2.0 + G[16 - j]) for j in range(9)])
    d = np.zeros(9)
    for i in range(7, -1, -1):
        s = sum(d[j] * (X[j] - X[i]) for j in range(i + 1, 9))
        d[i + 1] = (Ev[i] - s) / (X[i + 1] - X[i])
    terms = []
    for i in range(17):
        terms.append(("a", float(cG[i]), float(UG[i])))
    terms.append(("a", float(dstep), 1024.0))
    terms.append(("a", float(-dstep), 1025.0))
    for j in range(1, 9):
        terms.append(("u", float(256.0 * d[j]), float(-256.0 * X[j])))
    cjs = [(c, U) for (s, c, U) in terms if s == "u"]
    def Elr(u):
        return sum(c * max(-u - U, 0.0) for c, U in cjs)
    v1025 = Elr(-1025.0)
    v1026 = Elr(-1026.0)
    m = v1026 - v1025
    p = -v1025
    q = -v1026 - 2.0 * p
    terms.append(("u", float(p), 1024.0))
    terms.append(("u", float(q), 1025.0))
    def ge_y(u):
        acc = 0.5 + 0.5 * u
        for (s, c, U) in terms:
            acc += c * max((abs(u) if s == "a" else -u) - U, 0.0)
        return acc
    tref, cref = _pwl_terms()
    yref = cref + sum(c * max(sd * -740.0 - U, 0.0) for (c, U, sd) in tref)
    dfix = yref - ge_y(-740.0)
    return terms, float(dfix)

PWL_GE, PWL_DFIX = _pwl_terms_ge()
PWL_POOL = frozenset()



class Chan:
    def __init__(self, sem, inc):
        self.sem = sem
        self.inc = inc
        self.count = 0


class Tile:
    __slots__ = ("h", "name", "wr", "rd")
    def __init__(self, h, name):
        self.h = h
        self.name = name
        self.wr = None
        self.rd = {}
    def __getitem__(self, idx):
        return self.h[idx]


class Bld:
    ENGS = ("tensor", "vector", "scalar", "gpsimd", "sync")

    def __init__(self, nc):
        self.nc = nc
        self._cm = []
        self.q = {e: [] for e in self.ENGS}
        self.ch = {}
        for name, inc in [("t", 1), ("v", 1), ("a", 1), ("p", 1),
                          ("qs0", 16), ("qs1", 16), ("qs2", 16), ("qs3", 16),
                          ("w0", 16), ("w1", 16), ("w2", 16), ("w3", 16), ("cc", 1)]:
            cm = nc.semaphore("ch_" + name)
            sem = cm.__enter__()
            self._cm.append(cm)
            self.ch[name] = Chan(sem, inc)
        self.clock = {}
        self._eng2ch = {"tensor": "t", "vector": "v", "scalar": "a", "gpsimd": "p"}
        self._wrot = 0
        self._qrot = 0
        self.nins = 0

    def sb(self, name, p, f, dt):
        return Tile(self.nc.alloc_sbuf_tensor(name, [p, f], dt), name)

    def ps(self, name, p, f, dt=F32):
        return Tile(self.nc.alloc_psum_tensor(name, [p, f], dt), name)

    def _wait(self, eng, chkey, count):
        if count <= 0:
            return
        if eng == "tensor" and chkey == "t":
            return
        key = (eng, chkey)
        if self.clock.get(key, 0) >= count:
            return
        self.clock[key] = count
        sem = self.ch[chkey].sem
        self.q[eng].append(lambda h, s=sem, c=count: h.wait_ge(s, c))

    def _pre(self, eng, reads, writes):
        for t in reads:
            if t.wr is not None:
                self._wait(eng, t.wr[0], t.wr[1])
        for t in writes:
            if t.wr is not None:
                self._wait(eng, t.wr[0], t.wr[1])
            for ck, cnt in t.rd.items():
                self._wait(eng, ck, cnt)

    def _post(self, chkey, reads, writes):
        ch = self.ch[chkey]
        ch.count += ch.inc
        for t in reads:
            t.rd[chkey] = ch.count
        for t in writes:
            t.wr = (chkey, ch.count)
            t.rd = {}

    def op(self, eng, fn, reads=(), writes=(), chkey=None):
        ck = chkey or self._eng2ch[eng]
        self._pre(eng, list(reads), list(writes))
        self._post(ck, list(reads), list(writes))
        sem, icr = self.ch[ck].sem, self.ch[ck].inc
        self.q[eng].append(lambda h, f=fn, s=sem, c=icr: f(h).then_inc(s, c))
        self.nins += 1

    def grp(self, eng, fns, reads=(), writes=(), chkey=None):
        ck = chkey or self._eng2ch[eng]
        self._pre(eng, list(reads), list(writes))
        self._post(ck, list(reads), list(writes))
        sem, icr = self.ch[ck].sem, self.ch[ck].inc
        for f in fns[:-1]:
            self.q[eng].append(f)
        self.q[eng].append(lambda h, f=fns[-1], s=sem, c=icr: f(h).then_inc(s, c))
        self.nins += len(fns)

    def dma(self, qeng, chkey, dst_ap, src_ap, reads=(), writes=()):
        if chkey == "qs":
            chkey = "qs" + str(self._qrot % 4)
            self._qrot += 1
        self._pre(qeng, list(reads), list(writes))
        self._wait(qeng, chkey, self.ch[chkey].count)
        self._post(chkey, list(reads), list(writes))
        sem = self.ch[chkey].sem
        self.q[qeng].append(
            lambda h, o=dst_ap, i=src_ap, s=sem: h.dma_start(o, i).then_inc(s, 16))
        self.nins += 1

    def wload(self, dst_tile, src_ap, dst_ap=None):
        ck = "w" + str(self._wrot % 4)
        self._wrot += 1
        self.dma("gpsimd", ck,
                 dst_ap if dst_ap is not None else dst_tile[:, :], src_ap,
                 writes=[dst_tile])

    def barrier(self, eng, chkey):
        self._wait(eng, chkey, self.ch[chkey].count)

    def emit(self):
        with self.nc.Block() as block:
            for e in self.ENGS:
                if not self.q[e]:
                    continue
                dec = getattr(block, e)
                def section(h, q=tuple(self.q[e])):
                    for f in q:
                        f(h)
                dec(section)


def _const_layout():
    cvals = []
    def C(v):
        cvals.append(float(v))
        return len(cvals) - 1
    C_EXPS = C(-20.0)
    C_G8 = C(-8.0)
    C_INV256 = C(2.0 ** -8)
    C_EPS = C(1e-5)
    C_256 = C(256.0)
    pwl_cols = []
    for (srt, c, U) in PWL_GE:
        pwl_cols.append((C(abs(c)), C(-abs(c) * U), 1.0 if c > 0 else -1.0))
    return cvals, C_EXPS, C_G8, C_INV256, C_EPS, C_256, pwl_cols


VB_OUT = 0
VB2L0 = 8
VGAM1 = 16
VSHB = 24
VGAMF = 32
VBETF = 40


def build_program(part, stop=None):
    nc = bass.Bass("TRN2", target_bir_lowering=False)
    B = Bld(nc)
    do_l0 = part in ("l0", "full")
    do_l1 = part in ("l1", "full")

    cvals, C_EXPS, C_G8, C_INV256, C_EPS, C_256, pwl_cols = _const_layout()
    NCON = max(64, len(cvals))

    din = lambda n, s, dt=F32: nc.dram_tensor(n, list(s), dt, kind="ExternalInput")
    dout = lambda n, s, dt=F32: nc.dram_tensor(n, list(s), dt, kind="ExternalOutput")
    dint = lambda n, s, dt=F32: nc.dram_tensor(n, list(s), dt, kind="Internal")

    if do_l0:
        x_in = din("xown", (CH, 128, T))
        xoth = din("xoth", (CH, 128, T))
        wq_g = din("wq_g", (4, 8, 128, 128))
        wkvT = din("wkvT", (8, 128, 1024))
        woutT_g = din("woutT_g", (4, 128, 1024))
        w1_g = din("w1_g", (32, 8, 128, 128))
        w2_g = din("w2_g", (32, 128, 1024))
        b1l0_in = din("b1l0", (128, 32))
    if do_l1:
        qkva_g = din("qkva_g", (24, 8, 128, 128))
        gate_g = din("gate_g", (8, 128, 8))
        shared_g = din("shared_g", (8, 128, 1024))
        w1e_g = din("w1e_g", (NE, 32, 8, 128, 128))
        w2e_g = din("w2e_g", (NE, 32, 128, 1024))
        b1e_in = din("b1e", (128, 256))
        b2c_in = din("b2c", (128, 64))
        role_in = din("role", (128, 1))
        carry_in = dint("cc_in", (1, 1024))
        carry_out = dint("cc_out", (2, 1024))
        cb_d = dint("cb_d", (8, 1024))
        ssp = dint("sspill", (CH, 128, T))
        out_d = dout("out", (CH, 128, T))
    vecs_in = din("vecs", (128, 64))
    consts_in = din("consts", (128, NCON))
    ident_in = din("ident", (128, 128))
    if part == "l0":
        x1_d = dout("x1", (CH, 128, T))
    elif part == "l1":
        x1_d = din("x1", (CH, 128, T))
    else:
        x1_d = dint("x1", (CH, 128, T))

    sb, ps = B.sb, B.ps
    A = [sb(f"A{i}", 128, 1024, F32) for i in range(8)]
    Cp = [sb(f"C{i}", 128, 1024, F32) for i in range(8)]
    Dp = [sb(f"D{i}", 128, 1024, BF16) for i in range(16)]
    xnbf = [sb(f"xn{i}", 128, T, BF16) for i in range(CH)]
    wg = [sb(f"wg{i}", 128, 1024, BF16) for i in range(4)]
    ws = [sb(f"ws{i}", 128, 1024, BF16) for i in range(8)]
    r_m = sb("r_m", 1, 512, F32)
    r_v = sb("r_v", 1, 512, F32)
    r_rstd = sb("r_rstd", 1, 512, F32)
    r_mrstd = sb("r_mrstd", 1, 512, F32)
    r_rsqv = sb("r_rsqv", 1, 512, F32)
    r_mrsqv = sb("r_mrsqv", 1, 512, F32)
    r_t0 = sb("r_t0", 1, 512, F32)
    onesrow_f = sb("onesrow_f", 1, 128, F32)
    onesrow_bf = sb("onesrow_bf", 1, 512, BF16)
    onescol_f = sb("onescol_f", 128, 1, F32)
    svecs = sb("svecs", 128, 64, F32)
    sconst = sb("sconst", 128, NCON, F32)
    sident_f = sb("sident_f", 128, 128, F32)
    zti = sb("zti", 128, 1024, I32)
    pacc = Cp[7]
    ozi = sb("ozi", 128, 512, I32)
    oz = sb("oz", 128, 512, F32)
    ou0 = sb("ou0", 128, 512, F32)
    od = sb("od", 128, 512, F32)
    ot1 = sb("ot1", 128, 512, F32)
    oq = sb("oq", 128, 512, F32)
    og1 = sb("og1", 128, 512, F32)
    ek = sb("ek", 128, 512, F32)
    krec = sb("krec", 128, 8, F32)
    kbf = sb("kbf", 128, 512, BF16)
    vbf = sb("vbf", 128, 512, BF16)

    ctx_bd = sb("ctx_bd", 128, 512, BF16)
    qbf = [sb(f"qbf{m}", 128, 512, BF16) for m in range(4)]
    aobf = [sb(f"aobf{p}", 128, 512, BF16) for p in range(4)]
    if do_l0:
        b1cc = sb("b1cc", 128, 32, F32)
        b2cc = sb("b2cc", 128, 8, F32)
    if do_l1:
        b1e_s = sb("b1e_s", 128, 256, F32)
        b2c_s = sb("b2c_s", 128, 64, F32)
        role_sb = sb("role_sb", 128, 1, F32)
        carry_sb = sb("carry_sb", 128, 8, F32)
        carryin_sb = sb("carryin_sb", 128, 8, F32)
        comb = [sb(f"comb{tc}", 128, 8, F32) for tc in range(8)]
        cbstage = sb("cbstage", 8, 1024, F32)
        cbrow_bf = sb("cbrow_bf", 1, 1024, BF16)
        cbb = A[3]
        gx = sb("gx", 128, 64, BF16)
        p8 = sb("p8", 128, 8, F32)
        p8b = sb("p8b", 128, 8, F32)
        m1c = sb("m1c", 128, 1, F32)
        m2c = sb("m2c", 128, 1, F32)
        gf = A[4]
        gf2 = A[5]
    PS = [ps(f"ps{i}", 128, 1024) for i in range(4)]

    def cc(i):
        return sconst[:, i:i + 1]

    class DTile:
        __slots__ = ("name", "wr", "rd")
        def __init__(self, name):
            self.name = name
            self.wr = None
            self.rd = {}
    dT = {}
    def dt(name):
        if name not in dT:
            dT[name] = DTile(name)
        return dT[name]

    B.dma("sync", "qs", svecs[:, :], vecs_in[:, :], writes=[svecs])
    B.dma("sync", "qs", sconst[:, :], consts_in[:, :], writes=[sconst])
    B.dma("sync", "qs", sident_f[:, :], ident_in[:, :], writes=[sident_f])
    B.op("gpsimd", lambda h: h.memset(onesrow_f[:, :], 1.0), writes=[onesrow_f])
    B.op("gpsimd", lambda h: h.memset(onesrow_bf[:, :], 1.0), writes=[onesrow_bf])
    B.op("gpsimd", lambda h: h.memset(onescol_f[:, :], 1.0), writes=[onescol_f])
    if do_l0:
        B.dma("sync", "qs", b1cc[:, :], b1l0_in[:, :], writes=[b1cc])
        B.op("vector", lambda h: h.tensor_scalar(
            b1cc[:, :], b1cc[:, :], 256.0, 0.5, ALU.mult, ALU.add),
            reads=[b1cc], writes=[b1cc])
        B.op("vector", lambda h: h.tensor_scalar(
            b2cc[:, :], svecs[:, VB2L0:VB2L0 + 8], 256.0, 0.5, ALU.mult, ALU.add),
            reads=[svecs], writes=[b2cc])
    if do_l1:
        B.dma("sync", "qs", b1e_s[:, :], b1e_in[:, :], writes=[b1e_s])
        B.dma("sync", "qs", b2c_s[:, :], b2c_in[:, :], writes=[b2c_s])
        B.dma("sync", "qs", role_sb[:, :], role_in[:, :], writes=[role_sb])
        for k in range(8):
            B.dma("gpsimd", "w3", gx[:, k * 8:(k + 1) * 8], gate_g[k], writes=[gx])

    def stats_tile(xts, w, sq_slots, with_rsqv=False):
        tiles = [t for t, _ in xts]
        fns = [lambda h, t=t, sl=sl, i=i: h.matmul(
            PS[3].h[0:1, 0:w], onescol_f[:, :], t.h[sl],
            start=(i == 0), stop=(i == 7)) for i, (t, sl) in enumerate(xts)]
        B.grp("tensor", fns, reads=[onescol_f] + tiles, writes=[PS[3]])
        B.op("vector", lambda h: h.tensor_scalar(
            r_m[0:1, 0:w], PS[3][0:1, 0:w], 1.0 / D, None, ALU.mult),
            reads=[PS[3]], writes=[r_m])
        for i in range(8):
            t, sl = xts[i]
            st, ssl = sq_slots[i]
            B.op("scalar", lambda h, t=t, sl=sl, st=st, ssl=ssl: h.activation(
                st.h[ssl], t.h[sl], AF.Square),
                reads=[t], writes=[st])
        fns = [lambda h, i=i: h.matmul(
            PS[3].h[0:1, 512:512 + w], onescol_f[:, :], sq_slots[i][0].h[sq_slots[i][1]],
            start=(i == 0), stop=(i == 7)) for i in range(8)]
        B.grp("tensor", fns, reads=[onescol_f] + [s for s, _ in sq_slots], writes=[PS[3]])
        B.op("vector", lambda h: h.tensor_tensor(
            r_t0[0:1, 0:w], r_m[0:1, 0:w], r_m[0:1, 0:w], ALU.mult),
            reads=[r_m], writes=[r_t0])
        B.op("vector", lambda h: h.scalar_tensor_tensor(
            r_v[0:1, 0:w], PS[3][0:1, 512:512 + w], 1.0 / D, r_t0[0:1, 0:w],
            ALU.mult, ALU.subtract),
            reads=[PS[3], r_t0], writes=[r_v])
        B.op("scalar", lambda h: h.activation(
            r_rstd[0:1, 0:w], r_v[0:1, 0:w], AF.Sqrt,
            bias=sconst[0:1, C_EPS:C_EPS + 1]),
            reads=[r_v, sconst], writes=[r_rstd])
        B.op("vector", lambda h: h.reciprocal(r_rstd[0:1, 0:w], r_rstd[0:1, 0:w]),
             reads=[r_rstd], writes=[r_rstd])
        B.op("vector", lambda h: h.tensor_tensor(
            r_mrstd[0:1, 0:w], r_m[0:1, 0:w], r_rstd[0:1, 0:w], ALU.mult),
            reads=[r_m, r_rstd], writes=[r_mrstd])
        if with_rsqv:
            B.op("scalar", lambda h: h.activation(
                r_rsqv[0:1, 0:w], r_v[0:1, 0:w], AF.Sqrt),
                reads=[r_v], writes=[r_rsqv])
            B.op("vector", lambda h: h.reciprocal(r_rsqv[0:1, 0:w], r_rsqv[0:1, 0:w]),
                 reads=[r_rsqv], writes=[r_rsqv])
            B.op("vector", lambda h: h.tensor_tensor(
                r_mrsqv[0:1, 0:w], r_m[0:1, 0:w], r_rsqv[0:1, 0:w], ALU.mult),
                reads=[r_m, r_rsqv], writes=[r_mrsqv])

    def bcast2(rowA, rowB, w, bank):
        B.grp("tensor", [lambda h: h.matmul(
            bank.h[:, 0:w], onesrow_f[0:1, 0:128], rowA.h[0:1, 0:w],
            start=True, stop=True)],
            reads=[onesrow_f, rowA], writes=[bank])
        B.grp("tensor", [lambda h: h.matmul(
            bank.h[:, 512:512 + w], onesrow_f[0:1, 0:128], rowB.h[0:1, 0:w],
            start=True, stop=True)],
            reads=[onesrow_f, rowB], writes=[bank])

    def ln_apply(xts, w, outs, bank, gamma_base=None, tmpA=None, tmpB=None):
        for i in range(8):
            xt, xsl = xts[i]
            ot, osl = outs[i]
            B.op("vector", lambda h, xt=xt, xsl=xsl: h.tensor_tensor(
                tmpA.h[:, 0:w], xt.h[xsl], bank.h[:, 0:w], ALU.mult),
                reads=[xt, bank], writes=[tmpA])
            if gamma_base is None:
                B.op("vector", lambda h, ot=ot, osl=osl: h.tensor_tensor(
                    ot.h[osl], tmpA.h[:, 0:w], bank.h[:, 512:512 + w], ALU.subtract),
                    reads=[tmpA, bank], writes=[ot])
            else:
                B.op("vector", lambda h: h.tensor_tensor(
                    tmpB.h[:, 0:w], tmpA.h[:, 0:w], bank.h[:, 512:512 + w], ALU.subtract),
                    reads=[tmpA, bank], writes=[tmpB])
                B.op("vector", lambda h, ot=ot, osl=osl, i=i: h.tensor_scalar(
                    ot.h[osl], tmpB.h[:, 0:w],
                    svecs[:, gamma_base + i:gamma_base + i + 1], None, ALU.mult),
                    reads=[tmpB, svecs], writes=[ot])

    def intfloor(zt, zsl, it, u0, d, t1, outt, osl, w):
        B.op("scalar", lambda h: h.activation(it[:, 0:w], zt.h[zsl], AF.Copy),
             reads=[zt], writes=[it])
        B.op("vector", lambda h: h.tensor_tensor(d[:, 0:w], zt.h[zsl], it[:, 0:w],
                                                 ALU.subtract),
             reads=[zt, it], writes=[d])
        B.op("vector", lambda h: h.scalar_tensor_tensor(
            t1[:, 0:w], d[:, 0:w], 1.0, it[:, 0:w], ALU.is_gt, ALU.add),
            reads=[d, it], writes=[t1])
        B.op("vector", lambda h: h.tensor_scalar(
            d[:, 0:w], d[:, 0:w], 0.0, -1.0, ALU.is_lt, ALU.mult),
            reads=[d], writes=[d])
        B.op("vector", lambda h: h.tensor_tensor(outt.h[osl], t1[:, 0:w], d[:, 0:w],
                                                 ALU.add),
             reads=[t1, d], writes=[outt])

    SL512 = (slice(0, 128), slice(0, 512))

    if do_l0:
        for tt in range(4):
            s0 = tt * 512
            for i in range(8):
                B.dma("sync", "qs", A[i][:, 0:512], x_in.ap()[i, :, s0:s0 + 512],
                      writes=[A[i]])
            xts = [(A[i], SL512) for i in range(8)]
            sqs = [(Cp[i], SL512) for i in range(8)]
            stats_tile(xts, 512, sqs)
            bcast2(r_rstd, r_mrstd, 512, PS[2])
            outs = [(xnbf[i], (slice(0, 128), slice(s0, s0 + 512))) for i in range(8)]
            ln_apply(xts, 512, outs, PS[2], tmpA=oz)

        if stop == "ph1":
            for i in range(8):
                B.dma("gpsimd", "w3", x1_d.ap()[i], xnbf[i][:, :], reads=[xnbf[i]])
            B.barrier("gpsimd", "w3")
            B.emit()
            return nc
        for k in range(8):
            B.wload(ws[k], wkvT[k])

        def kv_chunk(src_tiles, sls, first, last, kb):
            fns = [lambda h, kcc=kcc, kb=kb: h.matmul(
                kb.h[:, 0:512], src_tiles[kcc].h[sls[kcc]], ws[kcc][:, 0:512],
                start=(kcc == 0), stop=(kcc == 7)) for kcc in range(8)]
            B.grp("tensor", fns, reads=list(src_tiles) + ws[:8], writes=[kb])
            fns = [lambda h, kcc=kcc, kb=kb: h.matmul(
                kb.h[:, 512:1024], src_tiles[kcc].h[sls[kcc]], ws[kcc][:, 512:1024],
                start=(kcc == 0), stop=(kcc == 7)) for kcc in range(8)]
            B.grp("tensor", fns, reads=list(src_tiles) + ws[:8], writes=[kb])
            B.op("scalar", lambda h, kb=kb: h.activation(ek[:, :], kb.h[:, 0:512], AF.Exp,
                                                  bias=cc(C_EXPS)),
                 reads=[kb, sconst], writes=[ek])
            B.op("vector", lambda h: h.tensor_reduce(
                krec[:, :], ek.h.ap().rearrange("p (hh ww) -> p hh ww", hh=8),
                axis=AX.X, op=ALU.add),
                reads=[ek], writes=[krec])
            B.op("vector", lambda h: h.reciprocal(krec[:, :], krec[:, :]),
                 reads=[krec], writes=[krec])
            for hh in range(8):
                B.op("vector", lambda h, hh=hh: h.tensor_scalar(
                    kbf[:, hh * 64:(hh + 1) * 64], ek[:, hh * 64:(hh + 1) * 64],
                    krec[:, hh:hh + 1], None, ALU.mult),
                    reads=[ek, krec], writes=[kbf])
            B.op("scalar", lambda h, kb=kb: h.activation(vbf[:, :], kb.h[:, 512:1024], AF.Copy),
                 reads=[kb], writes=[vbf])
            fns = [lambda h, hh=hh: h.matmul(
                PS[1].h[0:64, hh * 64:(hh + 1) * 64], kbf[:, hh * 64:(hh + 1) * 64],
                vbf[:, hh * 64:(hh + 1) * 64],
                start=(first and hh == 0), stop=(last and hh == 7)) for hh in range(8)]
            B.grp("tensor", fns, reads=[kbf, vbf], writes=[PS[1]])

        for tc in range(16):
            sl = (slice(0, 128), slice(tc * 128, (tc + 1) * 128))
            kv_chunk(xnbf, [sl] * 8, tc == 0, False, PS[0] if tc % 2 == 0 else PS[2])
        for og in range(4):
            for i in range(8):
                B.dma("sync", "qs", A[i][:, 0:512], xoth.ap()[i, :, og * 512:(og + 1) * 512],
                      writes=[A[i]])
            xts = [(A[i], SL512) for i in range(8)]
            sqs = [(Cp[i], SL512) for i in range(8)]
            stats_tile(xts, 512, sqs)
            bcast2(r_rstd, r_mrstd, 512, PS[2])
            outs = [(Dp[i], SL512) for i in range(8)]
            ln_apply(xts, 512, outs, PS[2], tmpA=oz)
            for tc in range(4):
                sl = (slice(0, 128), slice(tc * 128, (tc + 1) * 128))
                kv_chunk(Dp[:8], [sl] * 8, False, og == 3 and tc == 3,
                         PS[0] if tc % 2 == 0 else PS[3])
        B.op("vector", lambda h: h.tensor_copy(oq.h[0:64, 0:512], PS[1][0:64, 0:512]),
             reads=[PS[1]], writes=[oq])
        B.op("vector", lambda h: h.memset(ctx_bd[:, :], 0.0), writes=[ctx_bd])
        for pr in range(4):
            B.op("vector", lambda h, pr=pr: h.tensor_copy(
                ctx_bd.h[0:64, pr * 128:pr * 128 + 64],
                oq.h[0:64, 2 * pr * 64:2 * pr * 64 + 64]),
                reads=[oq], writes=[ctx_bd])
            B.op("vector", lambda h, pr=pr: h.tensor_copy(
                ctx_bd.h[64:128, pr * 128 + 64:pr * 128 + 128],
                oq.h[0:64, (2 * pr + 1) * 64:(2 * pr + 1) * 64 + 64]),
                reads=[oq], writes=[ctx_bd])

        if stop == "ph2":
            B.op("vector", lambda h: h.tensor_copy(A[0][:, 0:512], ctx_bd[:, :]),
                 reads=[ctx_bd], writes=[A[0]])
            B.dma("sync", "qs", x1_d.ap()[0], A[0][:, 0:1024], reads=[A[0]])
            for qq in range(4):
                B.barrier("sync", "qs" + str(qq))
            B.emit()
            return nc
        for m in range(4):
            B.wload(ws[m], wq_g.ap()[m].rearrange("k p j -> p k j"), dst_ap=ws[m].h[:, :].rearrange("p (k j) -> p k j", k=8))
        for k in range(4):
            B.wload(ws[4 + k], woutT_g[k])

        NTG = 1 if stop == "tg1" else 4
        for tg in range(NTG):
            s0 = tg * 512
            def emit_outfloor(pair, zt):
                for hf in range(2):
                    hsl = slice(hf * 512, hf * 512 + 512)
                    B.op("scalar", lambda h, zt=zt, hsl=hsl: h.activation(
                        ozi[:, 0:512], zt[:, hsl], AF.Copy),
                        reads=[zt], writes=[ozi])
                    B.op("vector", lambda h, zt=zt, hsl=hsl: h.tensor_tensor(
                        oz[:, 0:512], zt[:, hsl], ozi[:, 0:512], ALU.subtract),
                        reads=[zt, ozi], writes=[oz])
                    B.op("vector", lambda h: h.scalar_tensor_tensor(
                        od[:, 0:512], oz[:, 0:512], 1.0, ozi[:, 0:512],
                        ALU.is_gt, ALU.add),
                        reads=[oz, ozi], writes=[od])
                    B.op("vector", lambda h, pair=pair, hsl=hsl: h.scalar_tensor_tensor(
                        Dp[pair][:, hsl], oz[:, 0:512], 0.0, od[:, 0:512],
                        ALU.is_lt, ALU.subtract),
                        reads=[oz, od], writes=[Dp[pair]])
            for pair in range(16):
                zt = A[0] if pair % 2 == 0 else Cp[6]
                hbank = PS[0] if pair % 2 == 0 else PS[3]
                for half in range(2):
                    hc = 2 * pair + half
                    wt = wg[hc % 4]
                    B.wload(wt, w1_g.ap()[hc].rearrange("k p j -> p k j"), dst_ap=wt.h[:, :].rearrange("p (k j) -> p k j", k=8))
                    bsl = slice(half * 512, half * 512 + 512)
                    fns = [lambda h, kcc=kcc, wt=wt, bsl=bsl, s0=s0, hbank=hbank: h.matmul(
                        hbank.h[:, bsl], wt[:, kcc * 128:(kcc + 1) * 128],
                        xnbf[kcc][:, s0:s0 + 512], start=(kcc == 0), stop=(kcc == 7))
                        for kcc in range(8)]
                    B.grp("tensor", fns, reads=[wt] + xnbf, writes=[hbank])
                    B.op("scalar", lambda h, bsl=bsl, hc=hc, hbank=hbank, zt=zt: h.activation(
                        zt[:, bsl], hbank.h[:, bsl], AF.Identity,
                        bias=b1cc[:, hc:hc + 1], scale=cc(C_256)),
                        reads=[hbank, b1cc, sconst], writes=[zt])
                B.op("scalar", lambda h, zt=zt: h.activation(
                    zti[:, :], zt[:, :], AF.Copy),
                    reads=[zt], writes=[zti])
                B.op("vector", lambda h, zt=zt: h.tensor_tensor(
                    A[7][:, :], zt[:, :], zti[:, :], ALU.subtract),
                    reads=[zt, zti], writes=[A[7]])
                for hf, mt in ((0, od), (1, ot1)):
                    hsl = slice(hf * 512, hf * 512 + 512)
                    B.op("vector", lambda h, mt=mt, hsl=hsl: h.scalar_tensor_tensor(
                        mt[:, 0:512], A[7][:, hsl], 1.0, zti[:, hsl],
                        ALU.is_gt, ALU.add),
                        reads=[A[7], zti], writes=[mt])
                for hf, mt in ((0, od), (1, ot1)):
                    hsl = slice(hf * 512, hf * 512 + 512)
                    B.op("vector", lambda h, mt=mt, hsl=hsl: h.scalar_tensor_tensor(
                        A[4][:, hsl], A[7][:, hsl], 0.0, mt[:, 0:512],
                        ALU.is_lt, ALU.subtract),
                        reads=[A[7], mt], writes=[A[4]])
                if pair >= 1:
                    emit_outfloor(pair - 1, Cp[6] if pair % 2 == 0 else A[0])
                B.op("scalar", lambda h: h.activation(
                    A[7][:, :], A[4][:, :], AF.Abs),
                    reads=[A[4]], writes=[A[7]])
                B.op("vector", lambda h: h.tensor_scalar(
                    A[6][:, :], A[4][:, :], 740.0, -PWL_DFIX, ALU.is_equal,
                    ALU.mult),
                    reads=[A[4]], writes=[A[6]])
                rrot = (A[1], A[2], A[3], Cp[5])
                dve_pos = False
                pool_sign = 0.0
                for ti, ((srt, _c, _U), (cci, cbi, sgn)) in enumerate(
                        zip(PWL_GE, pwl_cols)):
                    r = rrot[ti % 4]
                    src = A[7] if srt == "a" else A[4]
                    B.op("scalar", lambda h, cci=cci, cbi=cbi, r=r, src=src:
                        h.activation(r[:, :], src[:, :], AF.Relu,
                                     bias=cc(cbi), scale=cc(cci)),
                        reads=[src, sconst], writes=[r])
                    if ti in PWL_POOL:
                        if pool_sign == 0.0:
                            pool_sign = sgn
                            B.op("gpsimd", lambda h, r=r: h.tensor_copy(
                                pacc[:, :], r[:, :]),
                                reads=[r], writes=[pacc])
                        else:
                            op = ALU.add if sgn == pool_sign else ALU.subtract
                            B.op("gpsimd", lambda h, r=r, op=op: h.tensor_tensor(
                                pacc[:, :], pacc[:, :], r[:, :], op),
                                reads=[r], writes=[pacc])
                    elif sgn > 0:
                        if not dve_pos:
                            dve_pos = True
                            B.op("vector", lambda h, r=r: h.tensor_scalar(
                                A[5][:, :], r[:, :], 0.5, None, ALU.add),
                                reads=[r], writes=[A[5]])
                        else:
                            B.op("vector", lambda h, r=r: h.tensor_tensor(
                                A[5][:, :], A[5][:, :], r[:, :], ALU.add),
                                reads=[r], writes=[A[5]])
                    else:
                        B.op("vector", lambda h, r=r: h.tensor_tensor(
                            A[6][:, :], A[6][:, :], r[:, :], ALU.add),
                            reads=[r], writes=[A[6]])
                B.op("vector", lambda h: h.tensor_tensor(
                    A[3][:, :], A[5][:, :], A[6][:, :], ALU.subtract),
                    reads=[A[5], A[6]], writes=[A[3]])
                if pool_sign != 0.0:
                    pop = ALU.add if pool_sign > 0 else ALU.subtract
                    B.op("vector", lambda h, pop=pop: h.tensor_tensor(
                        A[3][:, :], A[3][:, :], pacc[:, :], pop),
                        reads=[pacc], writes=[A[3]])
                B.op("vector", lambda h, zt=zt: h.scalar_tensor_tensor(
                    zt[:, :], A[4][:, :], -0.5, A[3][:, :], ALU.mult, ALU.add),
                    reads=[A[4], A[3]], writes=[zt])
            emit_outfloor(15, Cp[6])
            w2bank = [PS[1], PS[1], PS[2], PS[2], PS[0], PS[0], PS[3], PS[3]]
            for hc in range(32):
                wt = wg[hc % 4]
                B.wload(wt, w2_g[hc])
                pairt = Dp[hc // 2]
                hsl = slice((hc % 2) * 512, (hc % 2) * 512 + 512)
                fns = []
                for dmc in range(8):
                    bank = w2bank[dmc]
                    bsl = slice((dmc % 2) * 512, (dmc % 2) * 512 + 512)
                    fns.append(lambda h, wt=wt, dmc=dmc, bank=bank, bsl=bsl,
                               pairt=pairt, hsl=hsl, hc=hc: h.matmul(
                        bank.h[:, bsl], wt[:, dmc * 128:(dmc + 1) * 128],
                        pairt.h[:, hsl], start=(hc == 0), stop=(hc == 31)))
                B.grp("tensor", fns, reads=[wt, pairt],
                      writes=[PS[0], PS[1], PS[2], PS[3]])
            for dmc in (4, 5, 6, 7, 0, 1, 2, 3):
                bank = w2bank[dmc]
                bsl = slice((dmc % 2) * 512, (dmc % 2) * 512 + 512)
                B.op("vector", lambda h, bank=bank, bsl=bsl, dmc=dmc: h.tensor_scalar(
                    oz[:, :], bank.h[:, bsl], -1.0, b2cc[:, dmc:dmc + 1],
                    ALU.mult, ALU.add),
                    reads=[bank, b2cc], writes=[oz])
                intfloor(oz, SL512, ozi, ou0, od, ot1, oq, SL512, 512)
                B.op("scalar", lambda h, dmc=dmc: h.activation(
                    Cp[dmc][:, 0:512], oq[:, 0:512], AF.Silu, scale=cc(C_INV256)),
                    reads=[oq, sconst], writes=[Cp[dmc]])
            for mc in range(4):
                bank = PS[0 if mc < 2 else 3]
                bsl = slice((mc % 2) * 512, (mc % 2) * 512 + 512)
                fns = [lambda h, kcc=kcc, mc=mc, bank=bank, bsl=bsl, s0=s0: h.matmul(
                    bank.h[:, bsl], ws[mc][:, kcc * 128:(kcc + 1) * 128],
                    xnbf[kcc][:, s0:s0 + 512], start=(kcc == 0), stop=(kcc == 7))
                    for kcc in range(8)]
                B.grp("tensor", fns, reads=[ws[mc]] + xnbf, writes=[bank])
                B.op("scalar", lambda h, mc=mc, bank=bank, bsl=bsl: h.activation(
                    qbf[mc][:, :], bank.h[:, bsl], AF.Copy),
                    reads=[bank], writes=[qbf[mc]])
            for pr in range(4):
                bank = PS[0 if pr < 2 else 3]
                bsl = slice((pr % 2) * 512, (pr % 2) * 512 + 512)
                B.grp("tensor", [lambda h, pr=pr, bank=bank, bsl=bsl: h.matmul(
                    bank.h[:, bsl], ctx_bd[:, pr * 128:(pr + 1) * 128], qbf[pr][:, :],
                    start=True, stop=True)],
                    reads=[ctx_bd, qbf[pr]], writes=[bank])
                B.op("scalar", lambda h, pr=pr, bank=bank, bsl=bsl: h.activation(
                    aobf[pr][:, :], bank.h[:, bsl], AF.Copy),
                    reads=[bank], writes=[aobf[pr]])
            for dmc in range(8):
                bank = PS[1 + dmc % 2]
                bsl = slice(0, 512) if dmc < 4 else slice(512, 1024)
                fns = [lambda h, kcc=kcc, dmc=dmc, bank=bank, bsl=bsl: h.matmul(
                    bank.h[:, bsl], ws[4 + kcc][:, dmc * 128:(dmc + 1) * 128],
                    aobf[kcc][:, :], start=(kcc == 0), stop=(kcc == 3))
                    for kcc in range(4)]
                B.grp("tensor", fns, reads=ws[4:8] + aobf, writes=[bank])
                B.dma("sync", "qs", A[7][:, 0:512], x_in.ap()[dmc, :, s0:s0 + 512],
                      writes=[A[7]])
                B.op("vector", lambda h, bank=bank, bsl=bsl, dmc=dmc: h.tensor_scalar(
                    oz[:, :], bank.h[:, bsl],
                    svecs[:, VB_OUT + dmc:VB_OUT + dmc + 1], None, ALU.add),
                    reads=[bank, svecs], writes=[oz])
                B.op("vector", lambda h, dmc=dmc: h.tensor_tensor(
                    ot1[:, :], oz[:, 0:512], Cp[dmc][:, 0:512], ALU.mult),
                    reads=[oz, Cp[dmc]], writes=[ot1])
                B.op("vector", lambda h: h.tensor_tensor(
                    og1[:, :], A[7][:, 0:512], ot1[:, 0:512], ALU.add),
                    reads=[A[7], ot1], writes=[og1])
                B.dma("sync", "qs", x1_d.ap()[dmc, :, s0:s0 + 512], og1[:, 0:512],
                      reads=[og1], writes=[dt("x1")])

    if stop == "tg1":
        for qq in range(4):
            B.barrier("sync", "qs" + str(qq))
        B.emit()
        return nc
    if do_l1:
        for tt in range(4):
            s0 = tt * 512
            for i in range(8):
                B.dma("sync", "qs", A[i][:, 0:512], x1_d.ap()[i, :, s0:s0 + 512],
                      reads=[dt("x1")], writes=[A[i]])
            xts = [(A[i], SL512) for i in range(8)]
            sqs = [(Cp[i], SL512) for i in range(8)]
            stats_tile(xts, 512, sqs, with_rsqv=True)
            bcast2(r_rstd, r_mrstd, 512, PS[2])
            outs = [(xnbf[i], (slice(0, 128), slice(s0, s0 + 512))) for i in range(8)]
            ln_apply(xts, 512, outs, PS[2], tmpA=oz)
            bcast2(r_rsqv, r_mrsqv, 512, PS[2])
            outs2 = [(Dp[2 * i + tt // 2],
                      (slice(0, 128), slice((tt % 2) * 512, (tt % 2) * 512 + 512)))
                     for i in range(8)]
            ln_apply(xts, 512, outs2, PS[2], gamma_base=VGAM1, tmpA=oz, tmpB=ou0)

        def xnn_sl(kc, a, b):
            half = a // 1024
            return Dp[2 * kc + half], (slice(0, 128), slice(a - half * 1024, b - half * 1024))

        def qkva_mm(wt, bank, bsl, a, b):
            fns = []
            rds = [wt]
            for kcc in range(8):
                t, sl = xnn_sl(kcc, a, b)
                rds.append(t)
                fns.append(lambda h, t=t, sl=sl, kcc=kcc: h.matmul(
                    bank.h[:, bsl], wt[:, kcc * 128:(kcc + 1) * 128], t.h[sl],
                    start=(kcc == 0), stop=(kcc == 7)))
            B.grp("tensor", fns, reads=rds, writes=[bank])

        def scan_phase(ccn, full):
            wt = wg[0]
            B.wload(wt, qkva_g.ap()[8 + ccn].rearrange("k p j -> p k j"), dst_ap=wt.h[:, :].rearrange("p (k j) -> p k j", k=8))
            wt2 = wg[1]
            B.wload(wt2, qkva_g.ap()[16 + ccn].rearrange("k p j -> p k j"), dst_ap=wt2.h[:, :].rearrange("p (k j) -> p k j", k=8))
            for tt in range(4):
                bsl = slice((tt % 2) * 512, (tt % 2) * 512 + 512)
                asl = slice((tt % 2) * 512, (tt % 2) * 512 + 512)
                qkva_mm(wt, PS[0], bsl, tt * 512, (tt + 1) * 512)
                B.op("scalar", lambda h, bsl=bsl, tt=tt, asl=asl: h.activation(
                    A[tt // 2].h[:, asl], PS[0].h[:, bsl], AF.Copy),
                    reads=[PS[0]], writes=[A[tt // 2]])
                qkva_mm(wt2, PS[1], bsl, tt * 512, (tt + 1) * 512)
                B.op("scalar", lambda h, bsl=bsl, tt=tt, asl=asl: h.activation(
                    A[2 + tt // 2].h[:, asl], PS[1].h[:, bsl], AF.Sigmoid),
                    reads=[PS[1]], writes=[A[2 + tt // 2]])
            B.op("vector", lambda h: h.tensor_tensor_scan(
                A[4][:, :], A[2][:, :], A[0][:, :], 0.0, ALU.mult, ALU.add),
                reads=[A[2], A[0]], writes=[A[4]])
            B.op("vector", lambda h: h.tensor_tensor_scan(
                A[5][:, :], A[3][:, :], A[1][:, :], A[4][:, 1023:1024],
                ALU.mult, ALU.add),
                reads=[A[3], A[1], A[4]], writes=[A[5]])
            if full:
                B.op("vector", lambda h: h.tensor_tensor_scan(
                    A[6][:, :], A[2][:, :], A[2][:, :], 1.0, ALU.mult, ALU.bypass),
                    reads=[A[2]], writes=[A[6]])
                B.op("vector", lambda h: h.tensor_tensor_scan(
                    A[7][:, :], A[3][:, :], A[3][:, :], A[6][:, 1023:1024],
                    ALU.mult, ALU.bypass),
                    reads=[A[3], A[6]], writes=[A[7]])

        for ccn in range(8):
            scan_phase(ccn, full=False)
            B.op("vector", lambda h, ccn=ccn: h.tensor_copy(
                carry_sb[:, ccn:ccn + 1], A[5][:, 1023:1024]),
                reads=[A[5]], writes=[carry_sb])
        for ccn in range(8):
            B.dma("gpsimd", "w3",
                  carry_in.ap().rearrange("o (c p) -> (o c) p", p=128)[ccn]
                  .rearrange("p -> p ()"),
                  carry_sb[:, ccn:ccn + 1], reads=[carry_sb], writes=[dt("cc_in")])
        B.op("gpsimd", lambda h: h.collective_compute(
            "AllGather", ALU.bypass,
            replica_groups=[[0, 1], [2, 3], [4, 5], [6, 7]],
            ins=[carry_in.ap().opt()], outs=[carry_out.ap().opt()]),
            reads=[dt("cc_in")], writes=[dt("cc_out")], chkey="cc")
        B.barrier("gpsimd", "cc")
        for ccn in range(8):
            B.dma("gpsimd", "w3", carryin_sb[:, ccn:ccn + 1],
                  carry_out.ap()[0:1].rearrange("o (c p) -> (o c) p", p=128)[ccn]
                  .rearrange("p -> p ()"),
                  reads=[dt("cc_out")], writes=[carryin_sb])
        B.op("vector", lambda h: h.tensor_scalar(
            carryin_sb[:, :], carryin_sb[:, :], role_sb[:, 0:1], None, ALU.mult),
            reads=[carryin_sb, role_sb], writes=[carryin_sb])

        for ccn in range(8):
            scan_phase(ccn, full=True)
            B.op("vector", lambda h, ccn=ccn: h.scalar_tensor_tensor(
                A[4][:, :], A[6][:, :], carryin_sb[:, ccn:ccn + 1], A[4][:, :],
                ALU.mult, ALU.add),
                reads=[A[6], carryin_sb], writes=[A[4]])
            B.op("vector", lambda h, ccn=ccn: h.scalar_tensor_tensor(
                A[5][:, :], A[7][:, :], carryin_sb[:, ccn:ccn + 1], A[5][:, :],
                ALU.mult, ALU.add),
                reads=[A[7], carryin_sb], writes=[A[5]])
            wt3 = wg[2]
            B.wload(wt3, qkva_g.ap()[ccn].rearrange("k p j -> p k j"), dst_ap=wt3.h[:, :].rearrange("p (k j) -> p k j", k=8))
            for tt in range(4):
                bsl = slice((tt % 2) * 512, (tt % 2) * 512 + 512)
                qkva_mm(wt3, PS[2], bsl, tt * 512, (tt + 1) * 512)
                yt = A[4 + tt // 2]
                ysl = slice((tt % 2) * 512, (tt % 2) * 512 + 512)
                B.op("vector", lambda h, bsl=bsl, yt=yt, ysl=ysl: h.tensor_tensor(
                    yt.h[:, ysl], PS[2].h[:, bsl], yt.h[:, ysl], ALU.mult),
                    reads=[PS[2], yt], writes=[yt])
            B.dma("sync", "qs", ssp.ap()[ccn, :, 0:1024], A[4][:, :],
                  reads=[A[4]], writes=[dt("ssp")])
            B.dma("sync", "qs", ssp.ap()[ccn, :, 1024:2048], A[5][:, :],
                  reads=[A[5]], writes=[dt("ssp")])

        for k in range(8):
            B.wload(ws[k], shared_g[k])
        for tg in range(2):
            t0 = tg * 1024
            for tq in range(8):
                tc = tg * 8 + tq
                gbank = PS[tq % 2]
                fns = [lambda h, kcc=kcc, tc=tc, gbank=gbank: h.matmul(
                    gbank.h[:, 0:8], xnbf[kcc][:, tc * 128:(tc + 1) * 128],
                    gx[:, kcc * 8:(kcc + 1) * 8], start=(kcc == 0), stop=(kcc == 7))
                    for kcc in range(8)]
                B.grp("tensor", fns, reads=xnbf + [gx], writes=[gbank])
                B.op("scalar", lambda h, gbank=gbank: h.activation(
                    p8[:, :], gbank.h[:, 0:8], AF.Exp, bias=cc(C_G8)),
                    reads=[gbank, sconst], writes=[p8])
                B.op("vector", lambda h: h.tensor_reduce(
                    m1c[:, :], p8[:, :], axis=AX.X, op=ALU.add),
                    reads=[p8], writes=[m1c])
                B.op("vector", lambda h: h.reciprocal(m1c[:, :], m1c[:, :]),
                     reads=[m1c], writes=[m1c])
                B.op("vector", lambda h: h.tensor_scalar(
                    p8[:, :], p8[:, :], m1c[:, 0:1], None, ALU.mult),
                    reads=[p8, m1c], writes=[p8])
                B.op("vector", lambda h: h.tensor_reduce(
                    m1c[:, :], p8[:, :], axis=AX.X, op=ALU.max),
                    reads=[p8], writes=[m1c])
                B.op("vector", lambda h: h.tensor_scalar(
                    p8b[:, :], p8[:, :], m1c[:, 0:1], -2.0, ALU.is_ge, ALU.mult),
                    reads=[p8, m1c], writes=[p8b])
                B.op("vector", lambda h: h.tensor_tensor(
                    p8b[:, :], p8[:, :], p8b[:, :], ALU.add),
                    reads=[p8, p8b], writes=[p8b])
                B.op("vector", lambda h: h.tensor_reduce(
                    m2c[:, :], p8b[:, :], axis=AX.X, op=ALU.max),
                    reads=[p8b], writes=[m2c])
                B.op("vector", lambda h: h.tensor_scalar(
                    p8b[:, :], p8[:, :], m2c[:, 0:1], None, ALU.is_ge),
                    reads=[p8, m2c], writes=[p8b])
                B.op("vector", lambda h: h.tensor_tensor(
                    p8b[:, :], p8b[:, :], p8[:, :], ALU.mult),
                    reads=[p8b, p8], writes=[p8b])
                B.op("vector", lambda h: h.tensor_reduce(
                    m2c[:, :], p8b[:, :], axis=AX.X, op=ALU.add),
                    reads=[p8b], writes=[m2c])
                B.op("vector", lambda h: h.reciprocal(m2c[:, :], m2c[:, :]),
                     reads=[m2c], writes=[m2c])
                B.op("vector", lambda h, tq=tq: h.tensor_scalar(
                    comb[tq][:, :], p8b[:, :], m2c[:, 0:1], None, ALU.mult),
                    reads=[p8b, m2c], writes=[comb[tq]])
            for tq in range(8):
                B.grp("tensor", [lambda h, tq=tq: h.transpose(
                    PS[3].h[0:8, 0:128], comb[tq][:, :], sident_f[:, :])],
                    reads=[comb[tq], sident_f], writes=[PS[3]])
                B.op("vector", lambda h, tq=tq: h.tensor_copy(
                    cbstage[:, tq * 128:(tq + 1) * 128], PS[3][0:8, 0:128]),
                    reads=[PS[3]], writes=[cbstage])
            B.dma("sync", "qs", cb_d[:, :], cbstage[:, :],
                  reads=[cbstage], writes=[dt("cb_d")])

            for hf in range(2):
                th0 = t0 + hf * 512
                for g in range(2):
                    for dq in range(4):
                        dmc = g * 4 + dq
                        bank = PS[dq]
                        bsl = slice(hf * 512, hf * 512 + 512)
                        fns = [lambda h, kcc=kcc, dmc=dmc, bank=bank, bsl=bsl,
                               th0=th0: h.matmul(
                            bank.h[:, bsl], ws[kcc][:, dmc * 128:(dmc + 1) * 128],
                            xnbf[kcc][:, th0:th0 + 512],
                            start=(kcc == 0), stop=(kcc == 7)) for kcc in range(8)]
                        B.grp("tensor", fns, reads=ws[:8] + xnbf, writes=[bank])
                        B.op("scalar", lambda h, bank=bank, bsl=bsl, dmc=dmc: h.activation(
                            gf[:, 0:512], bank.h[:, bsl], AF.Gelu,
                            bias=svecs[:, VSHB + dmc:VSHB + dmc + 1]),
                            reads=[bank, svecs], writes=[gf])
                        B.op("vector", lambda h, dmc=dmc, hf=hf: h.tensor_scalar(
                            Cp[dmc][:, hf * 512:hf * 512 + 512], gf[:, 0:512], 0.5,
                            None, ALU.mult),
                            reads=[gf], writes=[Cp[dmc]])

            for e in range(NE):
                B.dma("gpsimd", "w3", cbrow_bf[0:1, :], cb_d[e:e + 1, :],
                      reads=[dt("cb_d")], writes=[cbrow_bf])
                for hf2 in range(2):
                    B.grp("tensor", [lambda h, hf2=hf2: h.matmul(
                        PS[3].h[:, hf2 * 512:hf2 * 512 + 512], onesrow_bf[0:1, 0:128],
                        cbrow_bf[0:1, hf2 * 512:hf2 * 512 + 512],
                        start=True, stop=True)],
                        reads=[onesrow_bf, cbrow_bf], writes=[PS[3]])
                B.op("vector", lambda h: h.tensor_copy(cbb[:, :], PS[3][:, :]),
                     reads=[PS[3]], writes=[cbb])
                for half in range(2):
                    for hcq in range(16):
                        hc = half * 16 + hcq
                        wt = wg[hcq % 4]
                        abank = PS[hcq % 2]
                        B.wload(wt, w1e_g.ap()[e, hc].rearrange("k p j -> p k j"), dst_ap=wt.h[:, :].rearrange("p (k j) -> p k j", k=8))
                        for hf in range(2):
                            bsl = slice(hf * 512, hf * 512 + 512)
                            fns = [lambda h, kcc=kcc, wt=wt, bsl=bsl, hf=hf, t0=t0,
                                   abank=abank: h.matmul(
                                abank.h[:, bsl], wt[:, kcc * 128:(kcc + 1) * 128],
                                xnbf[kcc][:, t0 + hf * 512:t0 + hf * 512 + 512],
                                start=(kcc == 0), stop=(kcc == 7)) for kcc in range(8)]
                            B.grp("tensor", fns, reads=[wt] + xnbf, writes=[abank])
                        B.op("scalar", lambda h, hcq=hcq, hc=hc, e=e, abank=abank: h.activation(
                            Dp[hcq][:, :], abank[:, :], AF.Gelu,
                            bias=b1e_s[:, e * 32 + hc:e * 32 + hc + 1]),
                            reads=[abank, b1e_s], writes=[Dp[hcq]])
                    for g in range(2):
                        for hcq in range(16):
                            hc = half * 16 + hcq
                            wt = wg[(hcq + g * 2) % 4]
                            B.wload(wt, w2e_g.ap()[e, hc])
                            fns = []
                            for dq in range(4):
                                dmc = g * 4 + dq
                                for hf in range(2):
                                    fns.append(lambda h, wt=wt, dmc=dmc, dq=dq, hf=hf,
                                               hcq=hcq: h.matmul(
                                        PS[dq].h[:, hf * 512:hf * 512 + 512],
                                        wt[:, dmc * 128:(dmc + 1) * 128],
                                        Dp[hcq][:, hf * 512:hf * 512 + 512],
                                        start=(hcq == 0), stop=(hcq == 15)))
                            B.grp("tensor", fns,
                                  reads=[wt] + [Dp[q] for q in range(16)],
                                  writes=[PS[0], PS[1], PS[2], PS[3]])
                        for dq in range(4):
                            dmc = g * 4 + dq
                            B.op("vector", lambda h, dq=dq, dmc=dmc, e=e: h.scalar_tensor_tensor(
                                gf2[:, :], PS[dq].h[:, :],
                                b2c_s[:, e * 8 + dmc:e * 8 + dmc + 1],
                                cbb[:, :], ALU.add, ALU.mult),
                                reads=[PS[dq], b2c_s, cbb], writes=[gf2])
                            B.op("vector", lambda h, dmc=dmc: h.tensor_tensor(
                                Cp[dmc][:, :], Cp[dmc][:, :], gf2[:, :], ALU.add),
                                reads=[gf2], writes=[Cp[dmc]])

            for sub in range(2):
                u0 = sub * 512
                xts = [(Cp[i], (slice(0, 128), slice(u0, u0 + 512))) for i in range(8)]
                sqs = [(A[i // 2], (slice(0, 128), slice((i % 2) * 512, (i % 2) * 512 + 512)))
                       for i in range(8)]
                stats_tile(xts, 512, sqs)
                bcast2(r_rstd, r_mrstd, 512, PS[0])
                for ci in range(8):
                    B.op("vector", lambda h, ci=ci, u0=u0: h.tensor_tensor(
                        gf[:, 0:512], Cp[ci][:, u0:u0 + 512], PS[0][:, 0:512], ALU.mult),
                        reads=[Cp[ci], PS[0]], writes=[gf])
                    B.op("vector", lambda h: h.tensor_tensor(
                        gf[:, 0:512], gf[:, 0:512], PS[0][:, 512:1024], ALU.subtract),
                        reads=[gf, PS[0]], writes=[gf])
                    B.op("vector", lambda h, ci=ci: h.tensor_scalar(
                        gf[:, 0:512], gf[:, 0:512],
                        svecs[:, VGAMF + ci:VGAMF + ci + 1],
                        svecs[:, VBETF + ci:VBETF + ci + 1], ALU.mult, ALU.add),
                        reads=[gf, svecs], writes=[gf])
                    B.op("scalar", lambda h: h.activation(
                        gf2[:, 0:512], gf[:, 0:512], AF.Silu),
                        reads=[gf], writes=[gf2])
                    B.dma("sync", "qs", A[6][:, 0:512],
                          x1_d.ap()[ci, :, t0 + u0:t0 + u0 + 512],
                          reads=[dt("x1")], writes=[A[6]])
                    B.dma("sync", "qs", A[7][:, 0:512],
                          ssp.ap()[ci, :, t0 + u0:t0 + u0 + 512],
                          reads=[dt("ssp")], writes=[A[7]])
                    B.op("vector", lambda h: h.tensor_tensor(
                        gf2[:, 0:512], A[7][:, 0:512], gf2[:, 0:512], ALU.mult),
                        reads=[A[7], gf2], writes=[gf2])
                    B.op("vector", lambda h: h.tensor_tensor(
                        gf2[:, 0:512], A[6][:, 0:512], gf2[:, 0:512], ALU.add),
                        reads=[A[6], gf2], writes=[gf2])
                    B.dma("sync", "qs", out_d.ap()[ci, :, t0 + u0:t0 + u0 + 512],
                          gf2[:, 0:512], reads=[gf2])
    if do_l1 or part == "l0":
        for qq in range(4):
            B.barrier("sync", "qs" + str(qq))
    B.emit()
    return nc



def _shared_inputs(inp):
    f = np.float32
    asc = np.ascontiguousarray
    d = {}
    w_qkv = inp["L0_attn_w_qkv"]
    wqT = asc(w_qkv[:512].T)
    d["wq_g"] = asc(wqT.reshape(8, 128, 4, 128).transpose(2, 0, 1, 3))
    d["wkvT"] = asc(w_qkv[512:1536].T.reshape(8, 128, 1024))
    d["woutT_g"] = asc(inp["L0_attn_w_out"].T.reshape(4, 128, 1024))
    w1 = inp["L0_ffn_w1"]
    d["w1_g"] = asc(w1.reshape(8, 128, 32, 128).transpose(2, 0, 1, 3))
    d["w2_g"] = asc(inp["L0_ffn_w2"].reshape(32, 128, 1024))
    d["b1l0"] = asc(inp["L0_ffn_b1"].reshape(32, 128).T)
    d["qkva_g"] = asc(inp["L1_attn_W_qkva"].T.reshape(8, 128, 24, 128)
                      .transpose(2, 0, 1, 3))
    d["gate_g"] = asc(inp["L1_ffn_gate_W"].reshape(8, 128, 8))
    d["shared_g"] = asc(inp["L1_ffn_shared_W"].reshape(8, 128, 1024))
    d["w1e_g"] = asc(inp["L1_ffn_W1"].reshape(8, 8, 128, 32, 128)
                     .transpose(0, 3, 1, 2, 4))
    d["w2e_g"] = asc(inp["L1_ffn_W2"].reshape(8, 32, 128, 1024))
    d["b1e"] = asc(inp["L1_ffn_b1"].reshape(8, 32, 128).transpose(2, 0, 1)
                   .reshape(128, 256))
    d["b2c"] = asc(inp["L1_ffn_b2"].reshape(8, 8, 128).transpose(2, 0, 1)
                   .reshape(128, 64))
    vecs = np.zeros((128, 64), f)
    vecs[:, VB_OUT:VB_OUT + 8] = inp["L0_attn_b_out"].reshape(8, 128).T
    vecs[:, VB2L0:VB2L0 + 8] = inp["L0_ffn_b2"].reshape(8, 128).T
    vecs[:, VGAM1:VGAM1 + 8] = inp["L1_attn_gamma"].reshape(8, 128).T
    vecs[:, VSHB:VSHB + 8] = inp["L1_ffn_shared_b"].reshape(8, 128).T
    vecs[:, VGAMF:VGAMF + 8] = inp["L1_ffn_gamma"].reshape(8, 128).T
    vecs[:, VBETF:VBETF + 8] = inp["L1_ffn_beta"].reshape(8, 128).T
    d["vecs"] = vecs
    cvals = _const_layout()[0]
    NCON = max(64, len(cvals))
    consts = np.zeros((128, NCON), f)
    for i, v in enumerate(cvals):
        consts[:, i] = v
    d["consts"] = consts
    d["ident"] = np.eye(128, dtype=f)
    return {k: np.asarray(v, f) for k, v in d.items()}


_L0_KEYS = ("xown", "xoth", "wq_g", "wkvT", "woutT_g", "w1_g", "w2_g", "b1l0",
            "vecs", "consts", "ident")
_L1_KEYS = ("qkva_g", "gate_g", "shared_g", "w1e_g", "w2e_g", "b1e", "b2c",
            "role", "vecs", "consts", "ident", "x1")


def kernel(**inp):
    inp = {k: np.asarray(v) for k, v in inp.items()}
    shared = _shared_inputs(inp)
    x = np.asarray(inp["L0_attn_x"], np.float32)
    cores = []
    for c in range(8):
        b, hh = c // 2, c % 2
        xo = np.ascontiguousarray(x[b, hh * 2048:(hh + 1) * 2048, :].T
                                  .reshape(8, 128, T))
        xt = np.ascontiguousarray(x[b, (1 - hh) * 2048:(2 - hh) * 2048, :].T
                                  .reshape(8, 128, T))
        role = np.full((128, 1), float(hh), np.float32)
        cores.append(dict(xown=xo, xoth=xt, role=role))

    if PART == "full":
        nc = build_program("full")
        in_maps = []
        for c in range(8):
            m = dict(shared)
            m.update({k: v for k, v in cores[c].items()})
            in_maps.append(m)
        res = run_bass_kernel_spmd(nc, in_maps, core_ids=list(range(8)))
        outs = [r["out"] for r in res.results]
    else:
        nc0 = build_program("l0")
        in_maps = []
        for c in range(8):
            m = {k: shared[k] for k in _L0_KEYS if k in shared}
            m["xown"] = cores[c]["xown"]
            m["xoth"] = cores[c]["xoth"]
            in_maps.append(m)
        res0 = run_bass_kernel_spmd(nc0, in_maps, core_ids=list(range(8)))
        x1s = [np.asarray(r["x1"], np.float32) for r in res0.results]
        nc1 = build_program("l1")
        in_maps = []
        for c in range(8):
            m = {k: shared[k] for k in _L1_KEYS if k in shared}
            m["role"] = cores[c]["role"]
            m["x1"] = x1s[c]
            in_maps.append(m)
        res = run_bass_kernel_spmd(nc1, in_maps, core_ids=list(range(8)))
        outs = [r["out"] for r in res.results]

    full = np.empty((4, 4096, 1024), np.float32)
    for c in range(8):
        b, hh = c // 2, c % 2
        o = np.asarray(outs[c], np.float32).reshape(1024, T)
        full[b, hh * 2048:(hh + 1) * 2048, :] = o.T
    return full


if __name__ == "__main__":
    import sys
    part = sys.argv[1] if len(sys.argv) > 1 else "l0"
    nc = build_program(part)
    print("built", part)
